# version 05
# baseline (speedup 1.0000x reference)
import contextlib

import numpy as np

import concourse.bass as bass
import concourse.mybir as mybir
from concourse.bass_utils import run_bass_kernel_spmd

F32 = mybir.dt.float32
BF16 = mybir.dt.bfloat16
I32 = mybir.dt.int32
AF = mybir.ActivationFunctionType
ALU = mybir.AluOpType
AX = mybir.AxisListType.X

S, D, FF = 4096, 1024, 4096
NQT = 8
MASK15 = -32768
RB = 384.5


def build_nc(dbg=None):
    dbg = dbg or {}
    stop = dbg.get("stop")

    nc = bass.Bass("TRN2", target_bir_lowering=False,
                   detect_race_conditions=False)

    x_h = nc.dram_tensor("x", [S, D], F32, kind="ExternalInput")
    w_hs = {n: nc.dram_tensor(n, shp, F32, kind="ExternalInput")
            for n, shp in (("wq", [D, D]), ("wk", [D, D]), ("wv", [D, D]),
                           ("wo", [D, D]), ("wup", [FF, D]), ("wdn", [D, FF]))}
    y_h = nc.dram_tensor("y", [S, D], F32, kind="ExternalOutput")

    kind_scr = "ExternalOutput" if dbg.get("dump_scr") else "Internal"
    ctx_scr = nc.dram_tensor("ctx_scr", [8, 128, S], BF16, kind=kind_scr)
    wup_scr = nc.dram_tensor("wup_scr", [32, 128, 1024], BF16, kind=kind_scr)
    wdn_scr = nc.dram_tensor("wdn_scr", [32, 128, D], BF16, kind=kind_scr)
    rows_scr = nc.dram_tensor("rows_scr", [4, 512], F32)
    woo_scr = nc.dram_tensor("woo_scr", [64, 8 * D], BF16)
    if dbg.get("dump_scr"):
        sinv_dbg = nc.dram_tensor("sinv_dbg", [128, 32], F32,
                                  kind="ExternalOutput")

    cnt = {}
    maxwait = {}
    uctr = [0]

    def un(base):
        uctr[0] += 1
        return "%s_%d" % (base, uctr[0])

    def bump(nm, k=1):
        cnt[nm] = cnt.get(nm, 0) + k
        return cnt[nm]

    def cur(nm):
        return cnt.get(nm, 0)

    st = contextlib.ExitStack()
    with st:
        def sbt(n, sh, dt):
            return st.enter_context(nc.sbuf_tensor(n, sh, dt))

        wqT = sbt("wqT", [128, 8 * D], BF16)
        wkT = sbt("wkT", [128, 8 * D], BF16)
        wvT = sbt("wvT", [128, 8 * D], BF16)
        woT = sbt("woT", [128, 8 * D], BF16)
        woT_o = sbt("woT_o", [64, 8 * D], BF16)
        sc_grs = sbt("sc_grs", [128, 32], F32)
        sc_bq = sbt("sc_bq", [128, 32], F32)
        sc_sinv = sbt("sc_sinv", [128, 32], F32)
        sc_gxnf = sbt("sc_gxnf", [128, 32], F32)
        sc_bf = sbt("sc_bf", [128, 32], F32)
        sinv_rows = sbt("sinv_rows", [128, 32], F32)
        ident_bf = sbt("ident_bf", [128, 128], BF16)
        onesrow_bf = sbt("onesrow_bf", [128, 128], BF16)
        onescol_bf = sbt("onescol_bf", [128, 1], BF16)
        ones16 = sbt("ones16", [128, 8], BF16)
        ones16T = sbt("ones16T", [8, 128], BF16)
        m15 = sbt("m15", [128, 1], I32)
        scl = sbt("scl", [128, 16], F32)
        sp3 = sbt("sp3", [128, 4], BF16)
        cbias = sbt("cbias", [128, 8], F32)
        _cb_vals = [1e-5, 1e-6, 0.5, -0.5]
        for _i, _v in enumerate(_cb_vals):
            nc.const_aps.register(F32, _v, cbias[:, _i:_i + 1])

        semh = {nm: st.enter_context(nc.semaphore("s_" + nm))
                for nm in ("ldx", "ldw", "ldk", "ldv", "ldu", "ldd", "ldr",
                           "stg", "te", "se", "ve", "gp", "vs", "ss")}

        def SV(eng, ins):
            ins.then_inc(semh["vs"], 1)
            t = bump("vs")
            eng.wait_ge(semh["vs"], t)
            return ins

        def SS(eng, ins):
            ins.then_inc(semh["ss"], 1)
            t = bump("ss")
            eng.wait_ge(semh["ss"], t)
            return ins

        def split3(ve, src, d0, d1, d2, t0, t1):
            SV(ve, ve.tensor_copy(d0, src))
            SV(ve, ve.tensor_sub(t0, src, d0))
            SV(ve, ve.tensor_copy(d1, t0))
            SV(ve, ve.tensor_sub(t1, t0, d1))
            SV(ve, ve.tensor_copy(d2, t1))

        def incw(eng, ins, nm):
            inc(ins, nm)
            W(eng, nm, cur(nm))

        def W(eng, nm, tgt, dmasem=False):
            t = 16 * tgt if dmasem else tgt
            maxwait[nm] = max(maxwait.get(nm, 0), tgt)
            eng.wait_ge(semh[nm], t)

        def dma(eng, nm, out_ap, in_ap, slow=False):
            eng.dma_start(out_ap, in_ap,
                          allow_slow_non_contiguous=slow
                          ).then_inc(semh[nm], 16)
            bump(nm)

        def inc(ins, nm, k=1):
            ins.then_inc(semh[nm], k)
            bump(nm, k)

        def tr_pipeline(n_outer, src_fn, sink_fn, n_inner=8, store_fn=None,
                        wait_fn=None):
            pse = contextlib.ExitStack()
            with pse, nc.Block() as blk:
                pst = pse.enter_context(
                    nc.psum_tensor(un("pst"), [128, 2048], BF16))
                te_b, ve_b, stg_b = cur("te"), cur("ve"), cur("stg")

                @blk.tensor
                def _(te):
                    for i in range(n_outer):
                        if wait_fn is not None:
                            wait_fn(te, i)
                        if i >= 2:
                            W(te, "ve", ve_b + i - 1)
                        h = (i % 2) * 1024
                        for j in range(n_inner):
                            ins = te.transpose(
                                pst[:, h + j * 128:h + (j + 1) * 128],
                                src_fn(i, j), ident_bf[:, :])
                        inc(ins, "te")

                @blk.vector
                def _(ve):
                    for i in range(n_outer):
                        W(ve, "te", te_b + i + 1)
                        if store_fn is not None and i >= 2:
                            W(ve, "stg", stg_b + i - 1, True)
                        h = (i % 2) * 1024
                        inc(sink_fn(ve, i,
                                    pst[:, h:h + n_inner * 128]), "ve")

                if store_fn is not None:
                    @blk.gpsimd
                    def _(gp):
                        for i in range(n_outer):
                            W(gp, "ve", ve_b + i + 1)
                            store_fn(gp, i)
                        W(gp, "stg", cur("stg"), True)

        with nc.Block() as blk:
            @blk.gpsimd
            def _(gp):
                inc(gp.memset(ident_bf[:, :], 0.0), "gp")
                W(gp, "gp", cur("gp"))
                gp.affine_select(out=ident_bf[:, :], in_=ident_bf[:, :],
                                 compare_op=ALU.not_equal, fill=1.0, base=0,
                                 pattern=[[-1, 128]], channel_multiplier=1)
                inc(gp.memset(onesrow_bf[:, :], 0.0), "gp")
                W(gp, "gp", cur("gp"))
                gp.memset(onesrow_bf[0:1, :], 1.0)
                gp.memset(onescol_bf[:, :], 1.0)
                inc(gp.memset(ones16[:, :], 1.0), "gp")
                W(gp, "gp", cur("gp"))
                inc(gp.affine_select(out=ones16[:, :], in_=ones16[:, :],
                                     compare_op=ALU.is_ge, fill=0.0, base=0,
                                     pattern=[[-16, 8]],
                                     channel_multiplier=1), "gp")
                W(gp, "gp", cur("gp"))
                inc(gp.affine_select(out=ones16[:, :], in_=ones16[:, :],
                                     compare_op=ALU.is_ge, fill=0.0, base=15,
                                     pattern=[[16, 8]],
                                     channel_multiplier=-1), "gp")
                W(gp, "gp", cur("gp"))
                inc(gp.memset(ones16T[0:8, :], 1.0), "gp")
                W(gp, "gp", cur("gp"))
                inc(gp.affine_select(out=ones16T[0:8, :], in_=ones16T[0:8, :],
                                     compare_op=ALU.is_ge, fill=0.0, base=0,
                                     pattern=[[1, 128]],
                                     channel_multiplier=-16), "gp")
                W(gp, "gp", cur("gp"))
                gp.affine_select(out=ones16T[0:8, :], in_=ones16T[0:8, :],
                                 compare_op=ALU.is_ge, fill=0.0, base=15,
                                 pattern=[[-1, 128]], channel_multiplier=16)

            @blk.vector
            def _(ve):
                ve.memset(m15[:, :], MASK15)
                for _i, _v in enumerate(_cb_vals):
                    ve.memset(cbias[:, _i:_i + 1], _v)

        wst = contextlib.ExitStack()
        with wst:
            wf32 = wst.enter_context(nc.sbuf_tensor("wf32", [128, 8192], F32))
            a_scr = wst.enter_context(nc.sbuf_tensor("a_scr", [128, 8192], BF16))
            wqnt = wst.enter_context(nc.sbuf_tensor("wqnt", [128, 8192], BF16))
            wstage = wst.enter_context(nc.sbuf_tensor("wstage", [128, 2048], BF16))

            def absmean(rounds, has_clip):
                for r, (rap, rc) in enumerate(rounds):
                    with nc.Block() as blk:
                        @blk.sync
                        def _(sy, rap=rap, rc=rc):
                            dma(sy, "ldw",
                                wf32[:, :].rearrange("p (c n) -> p c n", c=rc),
                                rap)

                        @blk.vector
                        def _(ve, r=r):
                            W(ve, "ldw", cur("ldw"), True)
                            SV(ve, ve.reduce_sum(scl[:, 1:2], wf32[:, :],
                                                 axis=AX,
                                                 apply_absolute_value=True))
                            if r == 0:
                                SV(ve, ve.tensor_copy(scl[:, 0:1],
                                                      scl[:, 1:2]))
                            else:
                                SV(ve, ve.tensor_add(scl[:, 0:1], scl[:, 0:1],
                                                     scl[:, 1:2]))
                pse = contextlib.ExitStack()
                with pse, nc.Block() as blk:
                    psm = pse.enter_context(
                        nc.psum_tensor(un("psm"), [128, 8], F32))
                    te_b, ve_b = cur("te"), cur("ve")

                    @blk.tensor
                    def _(te):
                        W(te, "ve", ve_b + 1)
                        for j in range(3):
                            ins = te.matmul(psm[0:1, 0:1], sp3[:, j:j + 1],
                                            onescol_bf[:, :], start=(j == 0),
                                            stop=(j == 2))
                        inc(ins, "te")
                        W(te, "ve", ve_b + 2)
                        for j in range(3):
                            ins = te.matmul(psm[:, 1:2], onesrow_bf[0:1, :],
                                            sp3[0:1, j:j + 1], start=(j == 0),
                                            stop=(j == 2))
                        inc(ins, "te")

                    @blk.vector
                    def _(ve, n_r=len(rounds)):
                        split3(ve, scl[:, 0:1], sp3[:, 0:1], sp3[:, 1:2],
                               sp3[:, 2:3], scl[:, 10:11], scl[:, 11:12])
                        incw(ve, ve.engine_nop(), "ve")
                        W(ve, "te", te_b + 1)
                        SV(ve, ve.tensor_copy(scl[0:1, 9:10], psm[0:1, 0:1]))
                        split3(ve, scl[0:1, 9:10], sp3[0:1, 0:1],
                               sp3[0:1, 1:2], sp3[0:1, 2:3],
                               scl[0:1, 10:11], scl[0:1, 11:12])
                        incw(ve, ve.engine_nop(), "ve")
                        W(ve, "te", te_b + 2)
                        SV(ve, ve.tensor_copy(scl[:, 9:10], psm[:, 1:2]))
                        n_el = float(n_r * 128 * 8192)
                        SV(ve, ve.tensor_scalar_mul(scl[:, 2:3], scl[:, 9:10],
                                                    1.0 / n_el))
                        if has_clip:
                            SV(ve, ve.tensor_scalar_max(scl[:, 3:4],
                                                        scl[:, 2:3], 1e-5))
                            SV(ve, ve.tensor_scalar_mul(scl[:, 6:7],
                                                        scl[:, 3:4], -1.0))
                        else:
                            SV(ve, ve.tensor_copy(scl[:, 3:4], scl[:, 2:3]))
                            SV(ve, ve.memset(scl[:, 6:7], -1.0))
                        SV(ve, ve.tensor_scalar_mul(scl[:, 4:5], scl[:, 3:4],
                                                    0.5))
                        SV(ve, ve.tensor_scalar_mul(scl[:, 5:6], scl[:, 4:5],
                                                    -1.0))

            def quantize(rap, rc):
                with nc.Block() as blk:
                    @blk.sync
                    def _(sy):
                        dma(sy, "ldw",
                            wf32[:, :].rearrange("p (c n) -> p c n", c=rc), rap)

                    @blk.vector
                    def _(ve):
                        W(ve, "ldw", cur("ldw"), True)
                        SV(ve, ve.tensor_single_scalar(a_scr[:, :], wf32[:, :],
                                                       scl[:, 4:5], ALU.is_ge))
                        SV(ve, ve.scalar_tensor_tensor(wqnt[:, :], wf32[:, :],
                                                       scl[:, 5:6], a_scr[:, :],
                                                       ALU.is_le, ALU.subtract))
                        inc(ve.tensor_single_scalar(wqnt[:, :], wqnt[:, :],
                                                    scl[:, 6:7], ALU.mult),
                            "ve")
                return cur("ve")

            def rearr(w_h):
                return w_h.rearrange("(c p) d -> p c d", p=128)

            for nm, dstT in (("wk", wkT), ("wv", wvT), ("wq", wqT),
                             ("wo", woT)):
                absmean([(rearr(w_hs[nm]), 8)], True)
                qdone = quantize(rearr(w_hs[nm]), 8)
                tr_pipeline(
                    8,
                    lambda i, j: wqnt[:, j * 1024 + i * 128:
                                      j * 1024 + (i + 1) * 128],
                    lambda ve, i, ps, dstT=dstT: ve.tensor_copy(
                        dstT[:, i * 1024:(i + 1) * 1024], ps),
                    wait_fn=lambda te, i, qdone=qdone:
                        W(te, "ve", qdone) if i == 0 else None)

            with nc.Block() as blk:
                @blk.gpsimd
                def _(gp, ve_need=cur("ve")):
                    W(gp, "ve", ve_need)
                    dma(gp, "stg", woo_scr[:, :], woT[64:128, :])

                @blk.sync
                def _(sy, stg_need=cur("stg")):
                    W(sy, "stg", stg_need, True)
                    dma(sy, "ldw", woT_o[0:64, :], woo_scr[:, :])

            wup_rounds = [(rearr(w_hs["wup"])[:, 8 * r:8 * (r + 1), :], 8)
                          for r in range(4)]
            absmean(wup_rounds, False)
            for r in range(4):
                qdone = quantize(*wup_rounds[r])

                def up_sink(ve, i, ps):
                    h = (i % 2) * 1024
                    return ve.tensor_copy(wstage[:, h:h + 1024], ps)

                def up_store(gp, i, r=r):
                    h = (i % 2) * 1024
                    dma(gp, "stg", wup_scr[8 * r + i],
                        wstage[:, h:h + 1024])

                tr_pipeline(
                    8,
                    lambda i, j: wqnt[:, i * 1024 + j * 128:
                                      i * 1024 + (j + 1) * 128],
                    up_sink, store_fn=up_store,
                    wait_fn=lambda te, i, qdone=qdone:
                        W(te, "ve", qdone) if i == 0 else None)

            wdn_rounds = [(rearr(w_hs["wdn"])[:, 2 * r:2 * (r + 1), :], 2)
                          for r in range(4)]
            absmean(wdn_rounds, False)
            for r in range(4):
                qdone = quantize(*wdn_rounds[r])

                def dn_sink(ve, i, ps):
                    h = (i % 2) * 1024
                    return ve.tensor_copy(wstage[:, h:h + 256], ps[:, 0:256])

                def dn_store(gp, i, r=r):
                    h = (i % 2) * 1024
                    dma(gp, "stg",
                        wdn_scr[i, :, 2 * r * 128:(2 * r + 2) * 128],
                        wstage[:, h:h + 256])

                tr_pipeline(
                    32,
                    lambda i, j: wqnt[:, j * 4096 + i * 128:
                                      j * 4096 + (i + 1) * 128],
                    dn_sink, n_inner=2, store_fn=dn_store,
                    wait_fn=lambda te, i, qdone=qdone:
                        W(te, "ve", qdone) if i == 0 else None)

        if stop == "W":
            return nc

        bst = contextlib.ExitStack()
        with bst:
            def bbt(n, sh, dt):
                return bst.enter_context(nc.sbuf_tensor(n, sh, dt))

            xgrp = bbt("xgrp", [128, 4096], F32)
            xn_s = bbt("xn_s", [128, 1024], F32)
            xsq_s = bbt("xsq_s", [128, 1024], F32)
            zi = bbt("zi_b", [128, 1024], I32)
            zf = zi.bitcast(F32)
            xqg = bbt("xqg", [128, 4096], BF16)
            xqfm = bbt("xqfm", [128, 4096], BF16)
            qtk = bbt("qtk", [128, 4096], BF16)
            ktk = bbt("ktk", [128, 4096], BF16)
            vtk = bbt("vtk", [128, 4096], BF16)
            prod = bbt("prod", [128, 2048], F32)
            prodc = bbt("prodc", [128, 2048], BF16)
            sqs = bbt("sqs", [128, 256], F32)
            subs = bbt("subs", [128, 256], F32)
            pexp = bbt("pexp", [128, 256], F32)
            pn = bbt("pn", [128, 256], BF16)
            rmx = bbt("rmx", [128, 16], F32)
            den = bbt("den", [128, 16], F32)
            ctxu = bbt("ctxu", [128, 1024], F32)
            ctxg = bbt("ctxg", [128, 1024], F32)
            prodgb = bbt("prodgb", [128, 1024], BF16)
            ssqh = bbt("ssqh", [128, 16], F32)
            ssq_hl = bbt("ssq_hl", [128, 32], BF16)
            amx_bf = bbt("amx_bf", [128, 16], BF16)
            sm16 = bbt("sm16", [16, 64], F32)
            rs_hl_bf = bbt("rs_hl_bf", [16, 16], BF16)
            rsT_bf = bbt("rsT_bf", [8, 32], BF16)
            rs_tile = bbt("rs_tile", [128, 16], F32)
            xqc = bbt("xqc", [128, 1024], BF16)
            stage = bbt("stage", [128, 4096], BF16)
            sinv_all = bbt("sinv_all", [16, 256], F32)
            sinv_hl = bbt("sinv_hl", [16, 512], BF16)
            tmp16 = bbt("tmp16", [16, 256], F32)

            def ln_quant_block(g, xsrc, xq_dst):
                ve_b, se_b = cur("ve"), cur("se")
                with nc.Block() as blk:
                    @blk.vector
                    def _(ve):
                        def c(i):
                            return scl[:, i:i + 1]

                        for tt in range(4):
                            T = 4 * g + tt
                            xt = xsrc[:, tt * 1024:(tt + 1) * 1024]
                            SV(ve, ve.reduce_sum(c(0), xt, axis=AX))
                            SV(ve, ve.scalar_tensor_tensor(
                                xsq_s[:, :], xt, 0.0, xt, ALU.add, ALU.mult,
                                accum_out=c(1)))
                            SV(ve, ve.tensor_scalar_mul(c(0), c(0), 1.0 / D))
                            SV(ve, ve.tensor_scalar_mul(c(1), c(1), 1.0 / D))
                            SV(ve, ve.tensor_mul(c(2), c(0), c(0)))
                            incw(ve, ve.tensor_sub(c(1), c(1), c(2)),
                                 "ve")
                            W(ve, "se", se_b + 2 * tt + 1)
                            SV(ve, ve.tensor_scalar(xn_s[:, :], xt, c(0),
                                                    c(3), ALU.subtract,
                                                    ALU.mult))
                            SV(ve, ve.scalar_tensor_tensor(
                                xsq_s[:, :], xn_s[:, :], 0.0, xn_s[:, :],
                                ALU.add, ALU.mult, accum_out=c(2)))
                            SV(ve, ve.reduce_max(c(4), xn_s[:, :], axis=AX,
                                                 apply_absolute_value=True))
                            incw(ve, ve.tensor_scalar_max(c(5), c(2),
                                                          1e-30), "ve")
                            W(ve, "se", se_b + 2 * tt + 2)
                            SV(ve, ve.tensor_mul(c(4), c(4), c(6)))
                            SV(ve, ve.tensor_scalar_max(c(4), c(4), 1e-5))
                            SV(ve, ve.tensor_scalar_mul(
                                sc_sinv[:, T:T + 1], c(4), 1.0 / 127.0))
                            SV(ve, ve.reciprocal(c(5), c(4)))
                            SV(ve, ve.tensor_mul(c(5), c(5), c(6)))
                            SV(ve, ve.tensor_scalar_mul(c(5), c(5), 127.0))
                            SV(ve, ve.tensor_scalar_add(c(7), c(7), 1e-8))
                            SV(ve, ve.reciprocal(c(7), c(7)))
                            SV(ve, ve.tensor_mul(sc_grs[:, T:T + 1], c(3),
                                                 c(5)))
                            SV(ve, ve.tensor_mul(sc_gxnf[:, T:T + 1], c(3),
                                                 c(7)))
                            SV(ve, ve.tensor_mul(c(2), c(0),
                                                 sc_grs[:, T:T + 1]))
                            SV(ve, ve.tensor_scalar(sc_bq[:, T:T + 1],
                                                    c(2), -1.0, RB,
                                                    ALU.mult, ALU.add))
                            SV(ve, ve.tensor_mul(c(2), c(0),
                                                 sc_gxnf[:, T:T + 1]))
                            SV(ve, ve.tensor_scalar_mul(sc_bf[:, T:T + 1],
                                                        c(2), -1.0))
                            inc(ve.tensor_scalar(zf[:, :], xn_s[:, :],
                                                 c(5), RB, ALU.mult,
                                                 ALU.add), "ve")
                            W(ve, "ve", ve_b + 5 * tt + 3)
                            inc(ve.tensor_single_scalar(
                                zi[:, :], zi[:, :], m15[:, 0:1],
                                ALU.bitwise_and), "ve")
                            W(ve, "ve", ve_b + 5 * tt + 4)
                            inc(ve.tensor_scalar(
                                xq_dst[:, tt * 1024:(tt + 1) * 1024],
                                zf[:, :], 384.0, sc_sinv[:, T:T + 1],
                                ALU.subtract, ALU.mult), "ve")

                    @blk.scalar
                    def _(sc):
                        for tt in range(4):
                            W(sc, "ve", ve_b + 5 * tt + 1)
                            SS(sc, sc.activation(scl[:, 2:3], scl[:, 1:2],
                                                 AF.Ln, bias=1e-5,
                                                 scale=1.0))
                            incw(sc, sc.activation(scl[:, 3:4],
                                                    scl[:, 2:3], AF.Exp,
                                                    bias=0.0, scale=-0.5),
                                 "se")
                            W(sc, "ve", ve_b + 5 * tt + 2)
                            SS(sc, sc.activation(scl[:, 6:7], scl[:, 2:3],
                                                 AF.Ln, bias=1e-6,
                                                 scale=1.0 / D))
                            SS(sc, sc.activation(scl[:, 6:7], scl[:, 6:7],
                                                 AF.Exp, bias=0.0,
                                                 scale=-0.5))
                            SS(sc, sc.activation(scl[:, 7:8], scl[:, 5:6],
                                                 AF.Ln, bias=0.0,
                                                 scale=1.0 / D))
                            incw(sc, sc.activation(scl[:, 7:8],
                                                    scl[:, 7:8], AF.Exp,
                                                    bias=0.0, scale=0.5),
                                 "se")

            def h16(ap):
                return ap.rearrange("p (h n) -> p h n", h=16)

            def ht(ap):
                return ap.rearrange("p (h t) -> p h t", t=16)

            def colbc(ap2d, w):
                return ap2d.rearrange("p (h o) -> p h o", o=1).broadcast_to(
                    [128, 16, w])

            def segbc(seg):
                return seg.rearrange("p (o n) -> p o n", o=1).broadcast_to(
                    [128, 16, 64])

            stg_after_group = []
            for g in range(8):
                lnq_b = cur("ve")
                with nc.Block() as blk:
                    @blk.sync
                    def _(sy, g=g, lnq_b=lnq_b):
                        if g > 0:
                            W(sy, "ve", lnq_b)
                        dma(sy, "ldx",
                            xgrp[:, 0:4096].rearrange("p (t n) -> p t n",
                                                      t=4),
                            x_h.rearrange("(t p) d -> p t d", p=128)
                            [:, 4 * g:4 * (g + 1), :])

                    @blk.vector
                    def _(ve):
                        W(ve, "ldx", cur("ldx"), True)

                lnq_base = cur("ve")
                ln_quant_block(g, xgrp, xqg)

                tr_pipeline(
                    4,
                    lambda i, j: xqg[:, i * 1024 + j * 128:
                                     i * 1024 + (j + 1) * 128],
                    lambda ve, i, ps: ve.tensor_copy(
                        xqfm[:, :].rearrange("p (c n) -> p c n", c=8)
                        [:, :, i * 128:(i + 1) * 128],
                        ps.rearrange("p (c n) -> p c n", c=8)),
                    wait_fn=lambda te, i, b=lnq_base:
                        W(te, "ve", b + 5 * (i + 1)))

                pse = contextlib.ExitStack()
                with pse, nc.Block() as blk:
                    psp = [pse.enter_context(
                        nc.psum_tensor(un("psp"), [128, 512], F32))
                        for _ in range(4)]
                    te_b, ve_b = cur("te"), cur("ve")

                    @blk.tensor
                    def _(te):
                        for idx in range(24):
                            pi, rem = divmod(idx, 8)
                            tt, eh = divmod(rem, 2)
                            wT = (wqT, wkT, wvT)[pi]
                            if idx < 4:
                                W(te, "ve", ve_b)
                            else:
                                W(te, "ve", ve_b + idx - 4 + 1)
                            for dc in range(8):
                                ins = te.matmul(
                                    psp[idx % 4][:, :],
                                    xqfm[:, dc * 512 + tt * 128:
                                         dc * 512 + (tt + 1) * 128],
                                    wT[:, dc * 1024 + eh * 512:
                                       dc * 1024 + (eh + 1) * 512],
                                    start=(dc == 0), stop=(dc == 7))
                            inc(ins, "te")

                    @blk.vector
                    def _(ve):
                        for idx in range(24):
                            pi, rem = divmod(idx, 8)
                            tt, eh = divmod(rem, 2)
                            dst = (qtk, ktk, vtk)[pi]
                            W(ve, "te", te_b + idx + 1)
                            inc(ve.tensor_copy(
                                dst[:, tt * 1024 + eh * 512:
                                    tt * 1024 + (eh + 1) * 512],
                                psp[idx % 4][:, :]), "ve")
                vet_proj = cur("ve")

                for tt in range(4):
                    bt = g * 4 + tt
                    qti = qtk[:, tt * 1024:(tt + 1) * 1024]
                    kti = ktk[:, tt * 1024:(tt + 1) * 1024]
                    vti = vtk[:, tt * 1024:(tt + 1) * 1024]
                    pse = contextlib.ExitStack()
                    with pse, nc.Block() as blk:
                        psS = pse.enter_context(
                            nc.psum_tensor(un("psS"), [128, 8], F32))
                        psT = pse.enter_context(
                            nc.psum_tensor(un("psT"), [128, 128], BF16))
                        psR = pse.enter_context(
                            nc.psum_tensor(un("psR"), [128, 32], BF16))
                        psB = pse.enter_context(
                            nc.psum_tensor(un("psB"), [128, 16], F32))
                        psX = pse.enter_context(
                            nc.psum_tensor(un("psX"), [128, 1024], BF16))
                        ve_b, te_b, se_b = cur("ve"), cur("te"), cur("se")
                        stg_gate = (stg_after_group[g - 1]
                                    if (g > 0 and tt == 0) else None)

                        gp_b = cur("gp")

                        @blk.gpsimd
                        def _(gp, tt=tt, vti=vti):
                            W(gp, "ve", ve_b + 2)
                            for t in range(11, 16):
                                vrep = segbc(vti[:, t * 64:(t + 1) * 64])
                                pcol = ht(pn)[:, :, t:t + 1].broadcast_to(
                                    [128, 16, 64])
                                if t == 11:
                                    gp.tensor_tensor(h16(ctxg), vrep, pcol,
                                                     ALU.mult)
                                else:
                                    gp.tensor_tensor(h16(prodgb), vrep, pcol,
                                                     ALU.mult)
                                    ins = gp.tensor_tensor(
                                        ctxg[:, :], ctxg[:, :],
                                        prodgb[:, :], ALU.add)
                            inc(ins, "gp")

                        @blk.vector
                        def _(ve, tt=tt, bt=bt, qti=qti, kti=kti, vti=vti,
                              stg_gate=stg_gate):
                            for t in range(16):
                                ph = prod[:, (t % 2) * 1024:
                                          (t % 2) * 1024 + 1024]
                                ve.tensor_tensor(
                                    h16(ph), h16(qti),
                                    segbc(kti[:, t * 64:(t + 1) * 64]),
                                    ALU.mult)
                                ins = ve.reduce_sum(
                                    ht(sqs)[:, :, t:t + 1].rearrange(
                                        "p h o -> p (h o)"),
                                    h16(ph), axis=AX)
                                if t == 15:
                                    SV(ve, ins)
                            SV(ve, ve.reduce_max(rmx[:, :], ht(sqs), axis=AX))
                            inc(ve.tensor_tensor(ht(subs), ht(sqs),
                                                 colbc(rmx, 16),
                                                 ALU.subtract), "ve")
                            W(ve, "se", se_b + 1)
                            SV(ve, ve.reduce_sum(den[:, :], ht(pexp),
                                                 axis=AX))
                            SV(ve, ve.reciprocal(rmx[:, :], den[:, :]))
                            inc(ve.tensor_tensor(ht(pn), ht(pexp),
                                                 colbc(rmx, 16),
                                                 ALU.mult), "ve")
                            for t in range(11):
                                vrep = segbc(vti[:, t * 64:(t + 1) * 64])
                                pcol = ht(pn)[:, :, t:t + 1].broadcast_to(
                                    [128, 16, 64])
                                if t == 0:
                                    ve.tensor_tensor(h16(ctxu), vrep, pcol,
                                                     ALU.mult)
                                else:
                                    pc = prodc[:, (t % 2) * 1024:
                                               (t % 2) * 1024 + 1024]
                                    ve.tensor_tensor(h16(pc), vrep, pcol,
                                                     ALU.mult)
                                    ve.tensor_tensor(ctxu[:, :], ctxu[:, :],
                                                     pc, ALU.add)
                            W(ve, "gp", gp_b + 1)
                            ve.tensor_tensor(ctxu[:, :], ctxu[:, :],
                                             ctxg[:, :], ALU.add)
                            ve.tensor_tensor(prod[:, 0:1024], ctxu[:, :],
                                             ctxu[:, :], ALU.mult)
                            SV(ve, ve.reduce_sum(ssqh[:, :],
                                                 h16(prod[:, 0:1024]),
                                                 axis=AX))
                            SV(ve, ve.reduce_max(amx_bf[:, :], h16(ctxu),
                                                 axis=AX,
                                                 apply_absolute_value=True))
                            SV(ve, ve.tensor_copy(ssq_hl[:, 0:16],
                                                  ssqh[:, :]))
                            SV(ve, ve.tensor_sub(rmx[:, :], ssqh[:, :],
                                                 ssq_hl[:, 0:16]))
                            inc(ve.tensor_copy(ssq_hl[:, 16:32], rmx[:, :]),
                                "ve")
                            W(ve, "te", te_b + 2)
                            W(ve, "se", se_b + 2)
                            SV(ve, ve.reduce_max(
                                sm16[0:16, 0:8],
                                psT[0:16, :].rearrange("p (g j) -> p g j",
                                                       j=16), axis=AX))
                            SV(ve, ve.tensor_mul(sm16[0:16, 16:24],
                                                 sm16[0:16, 0:8],
                                                 sm16[0:16, 8:16]))
                            SV(ve, ve.tensor_scalar_max(sm16[0:16, 16:24],
                                                        sm16[0:16, 16:24],
                                                        1e-5))
                            SV(ve, ve.tensor_scalar_mul(
                                sinv_all[0:16, bt * 8:bt * 8 + 8],
                                sm16[0:16, 16:24], 1.0 / 127.0))
                            SV(ve, ve.reciprocal(sm16[0:16, 24:32],
                                                 sm16[0:16, 16:24]))
                            SV(ve, ve.tensor_mul(sm16[0:16, 32:40],
                                                 sm16[0:16, 24:32],
                                                 sm16[0:16, 8:16]))
                            SV(ve, ve.tensor_scalar_mul(sm16[0:16, 40:48],
                                                        sm16[0:16, 32:40],
                                                        127.0))
                            SV(ve, ve.tensor_copy(rs_hl_bf[0:16, 0:8],
                                                  sm16[0:16, 40:48]))
                            SV(ve, ve.tensor_sub(sm16[0:16, 48:56],
                                                 sm16[0:16, 40:48],
                                                 rs_hl_bf[0:16, 0:8]))
                            inc(ve.tensor_copy(rs_hl_bf[0:16, 8:16],
                                               sm16[0:16, 48:56]), "ve")
                            W(ve, "te", te_b + 3)
                            inc(ve.tensor_copy(rsT_bf[0:8, 0:32],
                                               psR[0:8, 0:32]), "ve")
                            W(ve, "te", te_b + 4)
                            SV(ve, ve.tensor_copy(rs_tile[:, :], psB[:, :]))
                            incw(ve, ve.tensor_tensor(h16(zf), h16(ctxu),
                                                      colbc(rs_tile, 64),
                                                      ALU.mult), "ve")
                            incw(ve, ve.tensor_scalar_add(zf[:, :], zf[:, :],
                                                          RB), "ve")
                            incw(ve, ve.tensor_single_scalar(
                                zi[:, :], zi[:, :], m15[:, 0:1],
                                ALU.bitwise_and), "ve")
                            inc(ve.tensor_single_scalar(
                                xqc[:, :], zf[:, :], 384.0,
                                ALU.subtract), "ve")
                            W(ve, "te", te_b + 5)
                            if stg_gate is not None:
                                W(ve, "stg", stg_gate, True)
                            inc(ve.tensor_copy(
                                stage[:, :].rearrange("p (c n) -> p c n",
                                                      c=8)
                                [:, :, tt * 128:(tt + 1) * 128],
                                psX[:, :].rearrange("p (c n) -> p c n",
                                                    c=8)), "ve")

                        @blk.scalar
                        def _(sc):
                            W(sc, "ve", ve_b + 1)
                            inc(sc.activation(pexp[:, :], subs[:, :], AF.Exp,
                                              bias=0.0, scale=0.125), "se")
                            W(sc, "te", te_b + 1)
                            SS(sc, sc.activation(sm16[0:16, 8:16],
                                                 psS[0:16, :], AF.Ln,
                                                 bias=1e-6,
                                                 scale=1.0 / 1024.0))
                            inc(sc.activation(sm16[0:16, 8:16],
                                              sm16[0:16, 8:16], AF.Exp,
                                              bias=0.0, scale=-0.5), "se")

                        @blk.tensor
                        def _(te):
                            W(te, "ve", ve_b + 3)
                            te.matmul(psS[0:16, 0:8], ssq_hl[:, 0:16],
                                      ones16[:, :], start=True, stop=False)
                            inc(te.matmul(psS[0:16, 0:8], ssq_hl[:, 16:32],
                                          ones16[:, :], start=False,
                                          stop=True), "te")
                            inc(te.transpose(psT[0:16, :], amx_bf[:, :],
                                             ident_bf[:, :]), "te")
                            W(te, "ve", ve_b + 4)
                            te.transpose(psR[0:8, 0:16], rs_hl_bf[0:16, 0:8],
                                         ident_bf[0:16, 0:16])
                            inc(te.transpose(psR[0:8, 16:32],
                                             rs_hl_bf[0:16, 8:16],
                                             ident_bf[0:16, 0:16]), "te")
                            W(te, "ve", ve_b + 5)
                            te.matmul(psB[:, :], ones16T[0:8, :],
                                      rsT_bf[0:8, 0:16], start=True,
                                      stop=False)
                            inc(te.matmul(psB[:, :], ones16T[0:8, :],
                                          rsT_bf[0:8, 16:32], start=False,
                                          stop=True), "te")
                            W(te, "ve", ve_b + 9)
                            for dc in range(8):
                                ins = te.transpose(
                                    psX[:, dc * 128:(dc + 1) * 128],
                                    xqc[:, dc * 128:(dc + 1) * 128],
                                    ident_bf[:, :])
                            inc(ins, "te")


                with nc.Block() as blk:
                    @blk.gpsimd
                    def _(gp, g=g, ve_need=cur("ve")):
                        W(gp, "ve", ve_need)
                        dma(gp, "stg",
                            ctx_scr[:, :, g * 512:(g + 1) * 512].transpose(
                                [1, 0, 2]),
                            stage[:, :].rearrange("p (c n) -> p c n", c=8))
                stg_after_group.append(cur("stg"))

            pse = contextlib.ExitStack()
            with pse, nc.Block() as blk:
                psF = pse.enter_context(
                    nc.psum_tensor("psF", [128, 64], BF16))
                ve_b, te_b = cur("ve"), cur("te")

                @blk.vector
                def _(ve):
                    SV(ve, ve.tensor_copy(sinv_hl[0:16, 0:256],
                                          sinv_all[0:16, :]))
                    SV(ve, ve.tensor_sub(tmp16[0:16, :], sinv_all[0:16, :],
                                         sinv_hl[0:16, 0:256]))
                    inc(ve.tensor_copy(sinv_hl[0:16, 256:512],
                                       tmp16[0:16, :]), "ve")
                    W(ve, "te", te_b + 1)
                    for hc in range(2):
                        dst = sinv_rows[:, :].rearrange(
                            "p (h c) -> p h c", c=2)[:, :, hc:hc + 1]\
                            .rearrange("p h o -> p (h o)")
                        SV(ve, ve.tensor_copy(dst, psF[:, hc * 16:
                                                       (hc + 1) * 16]))
                        SV(ve, ve.tensor_tensor(dst, dst,
                                                psF[:, 32 + hc * 16:
                                                    32 + (hc + 1) * 16],
                                                ALU.add))
                    inc(ve.engine_nop(), "ve")

                @blk.tensor
                def _(te):
                    W(te, "ve", ve_b + 1)
                    for hc in range(2):
                        te.transpose(psF[:, hc * 16:(hc + 1) * 16],
                                     sinv_hl[0:16, hc * 128:(hc + 1) * 128],
                                     ident_bf[0:16, 0:16])
                    for hc in range(2):
                        ins = te.transpose(
                            psF[:, 32 + hc * 16:32 + (hc + 1) * 16],
                            sinv_hl[0:16, 256 + hc * 128:
                                    256 + (hc + 1) * 128],
                            ident_bf[0:16, 0:16])
                    inc(ins, "te")

            if dbg.get("dump_scr"):
                with nc.Block() as blk:
                    @blk.gpsimd
                    def _(gp, ve_need=cur("ve")):
                        W(gp, "ve", ve_need)
                        dma(gp, "stg", sinv_dbg[:, :], sinv_rows[:, :])

        stg_bprime = cur("stg")
        if stop == "B":
            with nc.Block() as blk:
                @blk.gpsimd
                def _(gp):
                    W(gp, "stg", cur("stg"), True)
            _chk(maxwait, cnt)
            return nc

        qst = contextlib.ExitStack()
        with qst:
            def qb(n, sh, dt):
                return qst.enter_context(nc.sbuf_tensor(n, sh, dt))

            x_qt = qb("x_qt", [128, 4096], F32)
            ctx_fm = qb("ctx_fm", [64, 4096], BF16)
            ctx_sw = qb("ctx_sw", [64, 4096], BF16)
            tok2 = qb("tok2", [128, 2048], BF16)
            xnf_fm = qb("xnf_fm", [128, 4096], BF16)
            h_qt = qb("h_qt", [128, 8192], BF16)
            hsq = qb("hsq", [128, 512], BF16)
            wup_b = qb("wup_b", [128, 4096], BF16)
            wdn_b = qb("wdn_b", [128, 4096], BF16)
            out_sb = qb("out_sb", [128, 2048], F32)
            rowt = qb("rowt", [128, 512], F32)
            tmst = qb("tmst", [128, 8], F32)

            for qt in range(NQT):
                if stop is not None and isinstance(stop, int) and qt > stop:
                    break
                with nc.Block() as blk:
                    @blk.sync
                    def _(sy, qt=qt):
                        if qt == 0:
                            W(sy, "stg", stg_bprime, True)
                        dma(sy, "ldx",
                            x_qt[:, :].rearrange("p (t n) -> p t n", t=4),
                            x_h.rearrange("(t p) d -> p t d", p=128)
                            [:, 4 * qt:4 * (qt + 1), :])
                        dma(sy, "ldk", ctx_fm[0:64, :],
                            ctx_scr[qt][0:64, :])
                        dma(sy, "ldv", ctx_sw[0:64, :],
                            ctx_scr[qt][64:128, :])
                ldx_need = cur("ldx")

                for half in range(2):
                    pse = contextlib.ExitStack()
                    with pse, nc.Block() as blk:
                        PAt = pse.enter_context(
                            nc.psum_tensor(un("PAt"), [128, 2048], BF16))
                        PAf = PAt.bitcast(F32)
                        U0 = pse.enter_context(
                            nc.psum_tensor(un("U0"), [128, 512], F32))
                        U1 = pse.enter_context(
                            nc.psum_tensor(un("U1"), [128, 512], F32))
                        H1 = pse.enter_context(
                            nc.psum_tensor(un("H1"), [128, 512], F32))
                        ve_b, te_b, se_b = cur("ve"), cur("te"), cur("se")
                        ldu_b, ldd_b = cur("ldu"), cur("ldd")
                        stg_b = cur("stg")
                        ldk_need, ldv_need = qt + 1, qt + 1

                        @blk.sync
                        def _(sy, qt=qt, half=half):
                            for gi in range(16):
                                if gi >= 2:
                                    W(sy, "te", te_b + 4 * gi - 2)
                                dma(sy, "ldu",
                                    wup_b[:, (gi % 2) * 2048:
                                          (gi % 2) * 2048 + 2048].rearrange(
                                        "p (a n) -> p a n", a=2),
                                    wup_scr[2 * gi:2 * gi + 2].transpose(
                                        [1, 0, 2]))
                            for gi in range(16):
                                if gi >= 2:
                                    W(sy, "te", te_b + 68 + 2 * gi)
                                dma(sy, "ldd",
                                    wdn_b[:, (gi % 2) * 2048:
                                          (gi % 2) * 2048 + 2048].rearrange(
                                        "p (a n) -> p a n", a=2),
                                    wdn_scr[2 * gi:2 * gi + 2].transpose(
                                        [1, 0, 2]))
                            W(sy, "stg", stg_b + 1, True)
                            dma(sy, "ldr", tmst[:, 4 + 2 * half:6 + 2 * half],
                                rows_scr[1 + half, 0:256].rearrange(
                                    "(j p) -> p j", p=128), slow=True)

                        @blk.vector
                        def _(ve, qt=qt, half=half, ldx_need=ldx_need):
                            if half == 0:
                                W(ve, "ldx", ldx_need, True)
                            for tl in range(2):
                                tt = half * 2 + tl
                                T = 4 * qt + tt
                                inc(ve.tensor_scalar(
                                    tok2[:, tl * 1024:(tl + 1) * 1024],
                                    x_qt[:, tt * 1024:(tt + 1) * 1024],
                                    sc_gxnf[:, T:T + 1], sc_bf[:, T:T + 1],
                                    ALU.mult, ALU.add), "ve")
                            for tl in range(2):
                                W(ve, "te", te_b + tl + 1)
                                hh = tl * 1024
                                inc(ve.tensor_copy(
                                    xnf_fm[:, :].rearrange(
                                        "p (c n) -> p c n", c=8)
                                    [:, :, (half * 2 + tl) * 128:
                                     (half * 2 + tl + 1) * 128],
                                    PAt[:, hh:hh + 1024].rearrange(
                                        "p (c n) -> p c n", c=8)), "ve")
                            for fc in range(32):
                                W(ve, "se", se_b + fc + 1)
                                if fc >= 2:
                                    W(ve, "te", te_b + 2 * fc)
                                hs = h_qt[:, fc * 256:(fc + 1) * 256]
                                inc(ve.tensor_mul(
                                    hsq[:, (fc % 2) * 256:
                                        (fc % 2) * 256 + 256],
                                    hs, hs), "ve")
                            W(ve, "te", te_b + 66)
                            incw(ve, ve.tensor_scalar_max(
                                rowt[0:1, 0:256],
                                H1[0:1, half * 256:half * 256 + 256], 1e-35),
                                "ve")
                            W(ve, "se", se_b + 33)
                            SV(ve, ve.tensor_scalar_add(rowt[0:1, 0:256],
                                                        rowt[0:1, 0:256],
                                                        1e-8))
                            inc(ve.reciprocal(rowt[0:1, 0:256],
                                              rowt[0:1, 0:256]), "ve")
                            W(ve, "stg", stg_b, True)
                            for tl in range(2):
                                tt = half * 2 + tl
                                Tp = 4 * qt + tt
                                for eh in range(2):
                                    W(ve, "te", te_b + 67 + 2 * tl + eh)
                                    sl = out_sb[:, tl * 1024 + eh * 512:
                                                tl * 1024 + (eh + 1) * 512]
                                    inc(ve.scalar_tensor_tensor(
                                        sl, (U0, U1)[eh][:, :],
                                        sinv_rows[:, Tp:Tp + 1],
                                        x_qt[:, tt * 1024 + eh * 512:
                                             tt * 1024 + (eh + 1) * 512],
                                        ALU.mult, ALU.add), "ve")
                            W(ve, "te", te_b + 102)
                            W(ve, "ldr", cur("ldr"), True)
                            for tl in range(2):
                                for eh in range(2):
                                    ps = (PAf[:, 0:512], PAf[:, 512:1024],
                                          U0[:, :], U1[:, :])[tl * 2 + eh]
                                    inc(ve.scalar_tensor_tensor(
                                        out_sb[:, tl * 1024 + eh * 512:
                                               tl * 1024 + (eh + 1) * 512],
                                        ps,
                                        tmst[:, 4 + 2 * half + tl:
                                              5 + 2 * half + tl],
                                        out_sb[:, tl * 1024 + eh * 512:
                                               tl * 1024 + (eh + 1) * 512],
                                        ALU.mult, ALU.add), "ve")

                        @blk.tensor
                        def _(te, qt=qt, half=half):
                            for tl in range(2):
                                W(te, "ve", ve_b + tl + 1)
                                hh = tl * 1024
                                for dc in range(8):
                                    ins = te.transpose(
                                        PAt[:, hh + dc * 128:
                                            hh + (dc + 1) * 128],
                                        tok2[:, tl * 1024 + dc * 128:
                                             tl * 1024 + (dc + 1) * 128],
                                        ident_bf[:, :])
                                inc(ins, "te")
                            W(te, "ve", ve_b + 4)
                            for fc in range(32):
                                gi = fc // 2
                                W(te, "ldu", ldu_b + gi + 1, True)
                                if fc >= 2:
                                    W(te, "se", se_b + fc - 1)
                                ub = (U0, U1)[fc % 2]
                                base = (gi % 2) * 2048 + (fc % 2) * 1024
                                for dc in range(8):
                                    ins = te.matmul(
                                        ub[:, 0:256],
                                        wup_b[:, base + dc * 128:
                                              base + (dc + 1) * 128],
                                        xnf_fm[:, dc * 512 + half * 256:
                                               dc * 512 + half * 256 + 256],
                                        start=(dc == 0), stop=(dc == 7))
                                inc(ins, "te")
                                W(te, "ve", ve_b + 4 + fc + 1)
                                inc(te.matmul(
                                    H1[0:1, half * 256:half * 256 + 256],
                                    onescol_bf[:, :],
                                    hsq[:, (fc % 2) * 256:
                                        (fc % 2) * 256 + 256],
                                    start=(fc == 0), stop=(fc == 31)), "te")
                            W(te, "se", se_b + 32)
                            W(te, "ldk", ldk_need, True)
                            W(te, "ldv", ldv_need, True)
                            W(te, "ldw", cur("ldw"), True)
                            for tl in range(2):
                                tt = half * 2 + tl
                                hh, gh = tt // 2, tt % 2
                                csrc = ctx_fm if hh == 0 else ctx_sw
                                if tl == 1:
                                    W(te, "ve", ve_b + 40)
                                for eh in range(2):
                                    ps = (U0, U1)[eh]
                                    for j in range(16):
                                        lhsT = csrc.rearrange(
                                            "p (n j) -> p n j", j=16)[
                                            0:64,
                                            gh * 128:(gh + 1) * 128,
                                            j:j + 1].rearrange(
                                            "p n o -> p (n o)")
                                        wsrc = woT if j % 2 == 0 else woT_o
                                        ins = te.matmul(
                                            ps[:, :], lhsT,
                                            wsrc[0:64,
                                                 (j // 2) * 1024 + eh * 512:
                                                 (j // 2) * 1024 +
                                                 (eh + 1) * 512],
                                            start=(j == 0),
                                            stop=(j == 15))
                                    inc(ins, "te")
                            W(te, "ve", ve_b + 42)
                            for gi in range(16):
                                W(te, "ldd", ldd_b + gi + 1, True)
                                for f2 in range(2):
                                    fc = 2 * gi + f2
                                    base = (gi % 2) * 2048 + f2 * 1024
                                    for tl in range(2):
                                        for eh in range(2):
                                            ps = (PAf[:, 0:512],
                                                  PAf[:, 512:1024],
                                                  U0[:, :], U1[:, :])[
                                                      tl * 2 + eh]
                                            ins = te.matmul(
                                                ps,
                                                h_qt[:, fc * 256 + tl * 128:
                                                     fc * 256 +
                                                     (tl + 1) * 128],
                                                wdn_b[:, base + eh * 512:
                                                      base + (eh + 1) * 512],
                                                start=(fc == 0),
                                                stop=(fc == 31))
                                    inc(ins, "te")

                        @blk.scalar
                        def _(sc, qt=qt, half=half):
                            for fc in range(32):
                                W(sc, "te", te_b + 2 + 2 * fc + 1)
                                ub = (U0, U1)[fc % 2]
                                inc(sc.activation(
                                    h_qt[:, fc * 256:(fc + 1) * 256],
                                    ub[:, 0:256], AF.Gelu, bias=0.0,
                                    scale=1.0), "se")
                            W(sc, "ve", ve_b + 37)
                            SS(sc, sc.activation(rowt[0:1, 0:256],
                                                 rowt[0:1, 0:256], AF.Ln,
                                                 bias=0.0, scale=1.0 / FF))
                            incw(sc, sc.activation(rowt[0:1, 0:256],
                                                   rowt[0:1, 0:256], AF.Exp,
                                                   bias=0.0, scale=0.5), "se")

                        @blk.gpsimd
                        def _(gp, qt=qt, half=half):
                            W(gp, "ve", ve_b + 38)
                            dma(gp, "stg", rows_scr[1 + half, 0:256],
                                rowt[0:1, 0:256])
                            for tl in range(2):
                                tt = half * 2 + tl
                                W(gp, "ve", ve_b + 44 + 2 * tl)
                                dma(gp, "stg",
                                    y_h[(4 * qt + tt) * 128:
                                        (4 * qt + tt + 1) * 128, :],
                                    out_sb[:, tl * 1024:(tl + 1) * 1024])
                            W(gp, "stg", cur("stg"), True)

            with nc.Block() as blk:
                @blk.gpsimd
                def _(gp):
                    W(gp, "stg", cur("stg"), True)

    _chk(maxwait, cnt)
    return nc


def _chk(maxwait, cnt):
    bad = {k: (v, cnt.get(k, 0)) for k, v in maxwait.items()
           if v > cnt.get(k, 0)}
    if bad:
        raise RuntimeError("wait targets exceed final counts: %r" % bad)


_NC_CACHE = []


def kernel(L0_attn_x, L0_attn_wq, L0_attn_wk, L0_attn_wv, L0_attn_wo,
           L0_ffn_w_up, L0_ffn_w_down):
    if not _NC_CACHE:
        _NC_CACHE.append(build_nc())
    nc = _NC_CACHE[0]
    x = np.ascontiguousarray(L0_attn_x)
    in_maps = []
    for c in range(8):
        in_maps.append({
            "x": np.ascontiguousarray(x[c]),
            "wq": np.ascontiguousarray(L0_attn_wq),
            "wk": np.ascontiguousarray(L0_attn_wk),
            "wv": np.ascontiguousarray(L0_attn_wv),
            "wo": np.ascontiguousarray(L0_attn_wo),
            "wup": np.ascontiguousarray(L0_ffn_w_up),
            "wdn": np.ascontiguousarray(L0_ffn_w_down),
        })
    res = run_bass_kernel_spmd(nc, in_maps, core_ids=list(range(8)))
    return np.stack([r["y"] for r in res.results], axis=0)


# version 06 (final)
# speedup vs baseline: 1.0520x; 1.0520x over previous
import contextlib

import numpy as np

import concourse.bass as bass
import concourse.mybir as mybir
from concourse.bass_utils import run_bass_kernel_spmd

F32 = mybir.dt.float32
BF16 = mybir.dt.bfloat16
I32 = mybir.dt.int32
AF = mybir.ActivationFunctionType
ALU = mybir.AluOpType
AX = mybir.AxisListType.X

S, D, FF = 4096, 1024, 4096
NQT = 8
MASK15 = -32768
RB = 384.5


def build_nc(dbg=None):
    dbg = dbg or {}
    stop = dbg.get("stop")

    nc = bass.Bass("TRN2", target_bir_lowering=False,
                   detect_race_conditions=False)

    x_h = nc.dram_tensor("x", [S, D], F32, kind="ExternalInput")
    w_hs = {n: nc.dram_tensor(n, shp, F32, kind="ExternalInput")
            for n, shp in (("wq", [D, D]), ("wk", [D, D]), ("wv", [D, D]),
                           ("wo", [D, D]), ("wup", [FF, D]), ("wdn", [D, FF]))}
    y_h = nc.dram_tensor("y", [S, D], F32, kind="ExternalOutput")

    kind_scr = "ExternalOutput" if dbg.get("dump_scr") else "Internal"
    ctx_scr = nc.dram_tensor("ctx_scr", [8, 128, S], BF16, kind=kind_scr)
    wup_scr = nc.dram_tensor("wup_scr", [32, 128, 1024], BF16, kind=kind_scr)
    wdn_scr = nc.dram_tensor("wdn_scr", [32, 128, D], BF16, kind=kind_scr)
    rows_scr = nc.dram_tensor("rows_scr", [4, 512], F32)
    woo_scr = nc.dram_tensor("woo_scr", [64, 8 * D], BF16)
    if dbg.get("dump_scr"):
        sinv_dbg = nc.dram_tensor("sinv_dbg", [128, 32], F32,
                                  kind="ExternalOutput")

    cnt = {}
    maxwait = {}
    uctr = [0]

    def un(base):
        uctr[0] += 1
        return "%s_%d" % (base, uctr[0])

    def bump(nm, k=1):
        cnt[nm] = cnt.get(nm, 0) + k
        return cnt[nm]

    def cur(nm):
        return cnt.get(nm, 0)

    st = contextlib.ExitStack()
    with st:
        def sbt(n, sh, dt):
            return st.enter_context(nc.sbuf_tensor(n, sh, dt))

        wqT = sbt("wqT", [128, 8 * D], BF16)
        wkT = sbt("wkT", [128, 8 * D], BF16)
        wvT = sbt("wvT", [128, 8 * D], BF16)
        woT = sbt("woT", [128, 8 * D], BF16)
        woT_o = sbt("woT_o", [64, 8 * D], BF16)
        sc_grs = sbt("sc_grs", [128, 32], F32)
        sc_bq = sbt("sc_bq", [128, 32], F32)
        sc_sinv = sbt("sc_sinv", [128, 32], F32)
        sc_gxnf = sbt("sc_gxnf", [128, 32], F32)
        sc_bf = sbt("sc_bf", [128, 32], F32)
        sinv_rows = sbt("sinv_rows", [128, 32], F32)
        ident_bf = sbt("ident_bf", [128, 128], BF16)
        onesrow_bf = sbt("onesrow_bf", [128, 128], BF16)
        onescol_bf = sbt("onescol_bf", [128, 1], BF16)
        ones16 = sbt("ones16", [128, 8], BF16)
        ones16T = sbt("ones16T", [8, 128], BF16)
        m15 = sbt("m15", [128, 1], I32)
        scl = sbt("scl", [128, 16], F32)
        sp3 = sbt("sp3", [128, 4], BF16)
        cbias = sbt("cbias", [128, 8], F32)
        _cb_vals = [1e-5, 1e-6, 0.5, -0.5]
        for _i, _v in enumerate(_cb_vals):
            nc.const_aps.register(F32, _v, cbias[:, _i:_i + 1])

        semh = {nm: st.enter_context(nc.semaphore("s_" + nm))
                for nm in ("ldx", "ldw", "ldk", "ldv", "ldu", "ldd", "ldr",
                           "stg", "te", "se", "ve", "gp", "vs", "ss")}

        def SV(eng, ins):
            ins.then_inc(semh["vs"], 1)
            t = bump("vs")
            eng.wait_ge(semh["vs"], t)
            return ins

        def SS(eng, ins):
            ins.then_inc(semh["ss"], 1)
            t = bump("ss")
            eng.wait_ge(semh["ss"], t)
            return ins

        def split3(ve, src, d0, d1, d2, t0, t1):
            SV(ve, ve.tensor_copy(d0, src))
            SV(ve, ve.tensor_sub(t0, src, d0))
            SV(ve, ve.tensor_copy(d1, t0))
            SV(ve, ve.tensor_sub(t1, t0, d1))
            SV(ve, ve.tensor_copy(d2, t1))

        def incw(eng, ins, nm):
            inc(ins, nm)
            W(eng, nm, cur(nm))

        def W(eng, nm, tgt, dmasem=False):
            t = 16 * tgt if dmasem else tgt
            maxwait[nm] = max(maxwait.get(nm, 0), tgt)
            eng.wait_ge(semh[nm], t)

        def dma(eng, nm, out_ap, in_ap, slow=False):
            eng.dma_start(out_ap, in_ap,
                          allow_slow_non_contiguous=slow
                          ).then_inc(semh[nm], 16)
            bump(nm)

        def inc(ins, nm, k=1):
            ins.then_inc(semh[nm], k)
            bump(nm, k)

        def tr_pipeline(n_outer, src_fn, sink_fn, n_inner=8, store_fn=None,
                        wait_fn=None):
            pse = contextlib.ExitStack()
            with pse, nc.Block() as blk:
                pst = pse.enter_context(
                    nc.psum_tensor(un("pst"), [128, 2048], BF16))
                te_b, ve_b, stg_b = cur("te"), cur("ve"), cur("stg")

                @blk.tensor
                def _(te):
                    for i in range(n_outer):
                        if wait_fn is not None:
                            wait_fn(te, i)
                        if i >= 2:
                            W(te, "ve", ve_b + i - 1)
                        h = (i % 2) * 1024
                        for j in range(n_inner):
                            ins = te.transpose(
                                pst[:, h + j * 128:h + (j + 1) * 128],
                                src_fn(i, j), ident_bf[:, :])
                        inc(ins, "te")

                @blk.vector
                def _(ve):
                    for i in range(n_outer):
                        W(ve, "te", te_b + i + 1)
                        if store_fn is not None and i >= 2:
                            W(ve, "stg", stg_b + i - 1, True)
                        h = (i % 2) * 1024
                        inc(sink_fn(ve, i,
                                    pst[:, h:h + n_inner * 128]), "ve")

                if store_fn is not None:
                    @blk.gpsimd
                    def _(gp):
                        for i in range(n_outer):
                            W(gp, "ve", ve_b + i + 1)
                            store_fn(gp, i)
                        W(gp, "stg", cur("stg"), True)

        with nc.Block() as blk:
            @blk.gpsimd
            def _(gp):
                inc(gp.memset(ident_bf[:, :], 0.0), "gp")
                W(gp, "gp", cur("gp"))
                gp.affine_select(out=ident_bf[:, :], in_=ident_bf[:, :],
                                 compare_op=ALU.not_equal, fill=1.0, base=0,
                                 pattern=[[-1, 128]], channel_multiplier=1)
                inc(gp.memset(onesrow_bf[:, :], 0.0), "gp")
                W(gp, "gp", cur("gp"))
                gp.memset(onesrow_bf[0:1, :], 1.0)
                gp.memset(onescol_bf[:, :], 1.0)
                inc(gp.memset(ones16[:, :], 1.0), "gp")
                W(gp, "gp", cur("gp"))
                inc(gp.affine_select(out=ones16[:, :], in_=ones16[:, :],
                                     compare_op=ALU.is_ge, fill=0.0, base=0,
                                     pattern=[[-16, 8]],
                                     channel_multiplier=1), "gp")
                W(gp, "gp", cur("gp"))
                inc(gp.affine_select(out=ones16[:, :], in_=ones16[:, :],
                                     compare_op=ALU.is_ge, fill=0.0, base=15,
                                     pattern=[[16, 8]],
                                     channel_multiplier=-1), "gp")
                W(gp, "gp", cur("gp"))
                inc(gp.memset(ones16T[0:8, :], 1.0), "gp")
                W(gp, "gp", cur("gp"))
                inc(gp.affine_select(out=ones16T[0:8, :], in_=ones16T[0:8, :],
                                     compare_op=ALU.is_ge, fill=0.0, base=0,
                                     pattern=[[1, 128]],
                                     channel_multiplier=-16), "gp")
                W(gp, "gp", cur("gp"))
                gp.affine_select(out=ones16T[0:8, :], in_=ones16T[0:8, :],
                                 compare_op=ALU.is_ge, fill=0.0, base=15,
                                 pattern=[[-1, 128]], channel_multiplier=16)

            @blk.vector
            def _(ve):
                ve.memset(m15[:, :], MASK15)
                for _i, _v in enumerate(_cb_vals):
                    ve.memset(cbias[:, _i:_i + 1], _v)

        wst = contextlib.ExitStack()
        with wst:
            wf32 = wst.enter_context(nc.sbuf_tensor("wf32", [128, 8192], F32))
            a_scr = wst.enter_context(nc.sbuf_tensor("a_scr", [128, 8192], BF16))
            wqnt = wst.enter_context(nc.sbuf_tensor("wqnt", [128, 8192], BF16))
            wstage = wst.enter_context(nc.sbuf_tensor("wstage", [128, 2048], BF16))

            def absmean(rounds, has_clip):
                for r, (rap, rc) in enumerate(rounds):
                    with nc.Block() as blk:
                        @blk.sync
                        def _(sy, rap=rap, rc=rc):
                            dma(sy, "ldw",
                                wf32[:, :].rearrange("p (c n) -> p c n", c=rc),
                                rap)

                        @blk.vector
                        def _(ve, r=r):
                            W(ve, "ldw", cur("ldw"), True)
                            SV(ve, ve.reduce_sum(scl[:, 1:2], wf32[:, :],
                                                 axis=AX,
                                                 apply_absolute_value=True))
                            if r == 0:
                                SV(ve, ve.tensor_copy(scl[:, 0:1],
                                                      scl[:, 1:2]))
                            else:
                                SV(ve, ve.tensor_add(scl[:, 0:1], scl[:, 0:1],
                                                     scl[:, 1:2]))
                pse = contextlib.ExitStack()
                with pse, nc.Block() as blk:
                    psm = pse.enter_context(
                        nc.psum_tensor(un("psm"), [128, 8], F32))
                    te_b, ve_b = cur("te"), cur("ve")

                    @blk.tensor
                    def _(te):
                        W(te, "ve", ve_b + 1)
                        for j in range(3):
                            ins = te.matmul(psm[0:1, 0:1], sp3[:, j:j + 1],
                                            onescol_bf[:, :], start=(j == 0),
                                            stop=(j == 2))
                        inc(ins, "te")
                        W(te, "ve", ve_b + 2)
                        for j in range(3):
                            ins = te.matmul(psm[:, 1:2], onesrow_bf[0:1, :],
                                            sp3[0:1, j:j + 1], start=(j == 0),
                                            stop=(j == 2))
                        inc(ins, "te")

                    @blk.vector
                    def _(ve, n_r=len(rounds)):
                        split3(ve, scl[:, 0:1], sp3[:, 0:1], sp3[:, 1:2],
                               sp3[:, 2:3], scl[:, 10:11], scl[:, 11:12])
                        incw(ve, ve.engine_nop(), "ve")
                        W(ve, "te", te_b + 1)
                        SV(ve, ve.tensor_copy(scl[0:1, 9:10], psm[0:1, 0:1]))
                        split3(ve, scl[0:1, 9:10], sp3[0:1, 0:1],
                               sp3[0:1, 1:2], sp3[0:1, 2:3],
                               scl[0:1, 10:11], scl[0:1, 11:12])
                        incw(ve, ve.engine_nop(), "ve")
                        W(ve, "te", te_b + 2)
                        SV(ve, ve.tensor_copy(scl[:, 9:10], psm[:, 1:2]))
                        n_el = float(n_r * 128 * 8192)
                        SV(ve, ve.tensor_scalar_mul(scl[:, 2:3], scl[:, 9:10],
                                                    1.0 / n_el))
                        if has_clip:
                            SV(ve, ve.tensor_scalar_max(scl[:, 3:4],
                                                        scl[:, 2:3], 1e-5))
                            SV(ve, ve.tensor_scalar_mul(scl[:, 6:7],
                                                        scl[:, 3:4], -1.0))
                        else:
                            SV(ve, ve.tensor_copy(scl[:, 3:4], scl[:, 2:3]))
                            SV(ve, ve.memset(scl[:, 6:7], -1.0))
                        SV(ve, ve.tensor_scalar_mul(scl[:, 4:5], scl[:, 3:4],
                                                    0.5))
                        SV(ve, ve.tensor_scalar_mul(scl[:, 5:6], scl[:, 4:5],
                                                    -1.0))

            def quantize(rap, rc):
                with nc.Block() as blk:
                    @blk.sync
                    def _(sy):
                        dma(sy, "ldw",
                            wf32[:, :].rearrange("p (c n) -> p c n", c=rc), rap)

                    @blk.vector
                    def _(ve):
                        W(ve, "ldw", cur("ldw"), True)
                        SV(ve, ve.tensor_single_scalar(a_scr[:, :], wf32[:, :],
                                                       scl[:, 4:5], ALU.is_ge))
                        SV(ve, ve.scalar_tensor_tensor(wqnt[:, :], wf32[:, :],
                                                       scl[:, 5:6], a_scr[:, :],
                                                       ALU.is_le, ALU.subtract))
                        inc(ve.tensor_single_scalar(wqnt[:, :], wqnt[:, :],
                                                    scl[:, 6:7], ALU.mult),
                            "ve")
                return cur("ve")

            def rearr(w_h):
                return w_h.rearrange("(c p) d -> p c d", p=128)

            for nm, dstT in (("wk", wkT), ("wv", wvT), ("wq", wqT),
                             ("wo", woT)):
                absmean([(rearr(w_hs[nm]), 8)], True)
                qdone = quantize(rearr(w_hs[nm]), 8)
                tr_pipeline(
                    8,
                    lambda i, j: wqnt[:, j * 1024 + i * 128:
                                      j * 1024 + (i + 1) * 128],
                    lambda ve, i, ps, dstT=dstT: ve.tensor_copy(
                        dstT[:, i * 1024:(i + 1) * 1024], ps),
                    wait_fn=lambda te, i, qdone=qdone:
                        W(te, "ve", qdone) if i == 0 else None)

            with nc.Block() as blk:
                @blk.gpsimd
                def _(gp, ve_need=cur("ve")):
                    W(gp, "ve", ve_need)
                    dma(gp, "stg", woo_scr[:, :], woT[64:128, :])

                @blk.sync
                def _(sy, stg_need=cur("stg")):
                    W(sy, "stg", stg_need, True)
                    dma(sy, "ldw", woT_o[0:64, :], woo_scr[:, :])

            wup_rounds = [(rearr(w_hs["wup"])[:, 8 * r:8 * (r + 1), :], 8)
                          for r in range(4)]
            absmean(wup_rounds, False)
            for r in range(4):
                qdone = quantize(*wup_rounds[r])

                def up_sink(ve, i, ps):
                    h = (i % 2) * 1024
                    return ve.tensor_copy(wstage[:, h:h + 1024], ps)

                def up_store(gp, i, r=r):
                    h = (i % 2) * 1024
                    dma(gp, "stg", wup_scr[8 * r + i],
                        wstage[:, h:h + 1024])

                tr_pipeline(
                    8,
                    lambda i, j: wqnt[:, i * 1024 + j * 128:
                                      i * 1024 + (j + 1) * 128],
                    up_sink, store_fn=up_store,
                    wait_fn=lambda te, i, qdone=qdone:
                        W(te, "ve", qdone) if i == 0 else None)

            wdn_rounds = [(rearr(w_hs["wdn"])[:, 2 * r:2 * (r + 1), :], 2)
                          for r in range(4)]
            absmean(wdn_rounds, False)
            for r in range(4):
                qdone = quantize(*wdn_rounds[r])

                def dn_sink(ve, i, ps):
                    h = (i % 2) * 1024
                    return ve.tensor_copy(wstage[:, h:h + 256], ps[:, 0:256])

                def dn_store(gp, i, r=r):
                    h = (i % 2) * 1024
                    dma(gp, "stg",
                        wdn_scr[i, :, 2 * r * 128:(2 * r + 2) * 128],
                        wstage[:, h:h + 256])

                tr_pipeline(
                    32,
                    lambda i, j: wqnt[:, j * 4096 + i * 128:
                                      j * 4096 + (i + 1) * 128],
                    dn_sink, n_inner=2, store_fn=dn_store,
                    wait_fn=lambda te, i, qdone=qdone:
                        W(te, "ve", qdone) if i == 0 else None)

        if stop == "W":
            return nc

        bst = contextlib.ExitStack()
        with bst:
            def bbt(n, sh, dt):
                return bst.enter_context(nc.sbuf_tensor(n, sh, dt))

            xgrp = bbt("xgrp", [128, 4096], F32)
            xn_s = bbt("xn_s", [128, 1024], F32)
            xsq_s = bbt("xsq_s", [128, 1024], F32)
            zi = bbt("zi_b", [128, 1024], I32)
            zf = zi.bitcast(F32)
            xqg = bbt("xqg", [128, 4096], BF16)
            xqfm = bbt("xqfm", [128, 4096], BF16)
            qtk = bbt("qtk", [128, 4096], BF16)
            ktk = bbt("ktk", [128, 4096], BF16)
            vtk = bbt("vtk", [128, 4096], BF16)
            prod = bbt("prod", [128, 2048], F32)
            prodc = bbt("prodc", [128, 2048], BF16)
            sqs = bbt("sqs", [128, 256], F32)
            subs = bbt("subs", [128, 256], F32)
            pexp = bbt("pexp", [128, 256], F32)
            pn = bbt("pn", [128, 256], BF16)
            rmx = bbt("rmx", [128, 16], F32)
            den = bbt("den", [128, 16], F32)
            ctxu = bbt("ctxu", [128, 1024], F32)
            ssqh = bbt("ssqh", [128, 16], F32)
            ssq_hl = bbt("ssq_hl", [128, 32], BF16)
            amx_bf = bbt("amx_bf", [128, 16], BF16)
            sm16 = bbt("sm16", [16, 64], F32)
            rs_hl_bf = bbt("rs_hl_bf", [16, 16], BF16)
            rsT_bf = bbt("rsT_bf", [8, 32], BF16)
            rs_tile = bbt("rs_tile", [128, 16], F32)
            xqc = bbt("xqc", [128, 1024], BF16)
            stage = bbt("stage", [128, 4096], BF16)
            sinv_all = bbt("sinv_all", [16, 256], F32)
            sinv_hl = bbt("sinv_hl", [16, 512], BF16)
            tmp16 = bbt("tmp16", [16, 256], F32)

            def ln_quant_block(g, xsrc, xq_dst):
                ve_b, se_b = cur("ve"), cur("se")
                with nc.Block() as blk:
                    @blk.vector
                    def _(ve):
                        def c(i):
                            return scl[:, i:i + 1]

                        for tt in range(4):
                            T = 4 * g + tt
                            xt = xsrc[:, tt * 1024:(tt + 1) * 1024]
                            SV(ve, ve.reduce_sum(c(0), xt, axis=AX))
                            SV(ve, ve.scalar_tensor_tensor(
                                xsq_s[:, :], xt, 0.0, xt, ALU.add, ALU.mult,
                                accum_out=c(1)))
                            SV(ve, ve.tensor_scalar_mul(c(0), c(0), 1.0 / D))
                            SV(ve, ve.tensor_scalar_mul(c(1), c(1), 1.0 / D))
                            SV(ve, ve.tensor_mul(c(2), c(0), c(0)))
                            incw(ve, ve.tensor_sub(c(1), c(1), c(2)),
                                 "ve")
                            W(ve, "se", se_b + 2 * tt + 1)
                            SV(ve, ve.tensor_scalar(xn_s[:, :], xt, c(0),
                                                    c(3), ALU.subtract,
                                                    ALU.mult))
                            SV(ve, ve.scalar_tensor_tensor(
                                xsq_s[:, :], xn_s[:, :], 0.0, xn_s[:, :],
                                ALU.add, ALU.mult, accum_out=c(2)))
                            SV(ve, ve.reduce_max(c(4), xn_s[:, :], axis=AX,
                                                 apply_absolute_value=True))
                            incw(ve, ve.tensor_scalar_max(c(5), c(2),
                                                          1e-30), "ve")
                            W(ve, "se", se_b + 2 * tt + 2)
                            SV(ve, ve.tensor_mul(c(4), c(4), c(6)))
                            SV(ve, ve.tensor_scalar_max(c(4), c(4), 1e-5))
                            SV(ve, ve.tensor_scalar_mul(
                                sc_sinv[:, T:T + 1], c(4), 1.0 / 127.0))
                            SV(ve, ve.reciprocal(c(5), c(4)))
                            SV(ve, ve.tensor_mul(c(5), c(5), c(6)))
                            SV(ve, ve.tensor_scalar_mul(c(5), c(5), 127.0))
                            SV(ve, ve.tensor_scalar_add(c(7), c(7), 1e-8))
                            SV(ve, ve.reciprocal(c(7), c(7)))
                            SV(ve, ve.tensor_mul(sc_grs[:, T:T + 1], c(3),
                                                 c(5)))
                            SV(ve, ve.tensor_mul(sc_gxnf[:, T:T + 1], c(3),
                                                 c(7)))
                            SV(ve, ve.tensor_mul(c(2), c(0),
                                                 sc_grs[:, T:T + 1]))
                            SV(ve, ve.tensor_scalar(sc_bq[:, T:T + 1],
                                                    c(2), -1.0, RB,
                                                    ALU.mult, ALU.add))
                            SV(ve, ve.tensor_mul(c(2), c(0),
                                                 sc_gxnf[:, T:T + 1]))
                            SV(ve, ve.tensor_scalar_mul(sc_bf[:, T:T + 1],
                                                        c(2), -1.0))
                            inc(ve.tensor_scalar(zf[:, :], xn_s[:, :],
                                                 c(5), RB, ALU.mult,
                                                 ALU.add), "ve")
                            W(ve, "ve", ve_b + 5 * tt + 3)
                            inc(ve.tensor_single_scalar(
                                zi[:, :], zi[:, :], m15[:, 0:1],
                                ALU.bitwise_and), "ve")
                            W(ve, "ve", ve_b + 5 * tt + 4)
                            inc(ve.tensor_scalar(
                                xq_dst[:, tt * 1024:(tt + 1) * 1024],
                                zf[:, :], 384.0, sc_sinv[:, T:T + 1],
                                ALU.subtract, ALU.mult), "ve")

                    @blk.scalar
                    def _(sc):
                        for tt in range(4):
                            W(sc, "ve", ve_b + 5 * tt + 1)
                            SS(sc, sc.activation(scl[:, 2:3], scl[:, 1:2],
                                                 AF.Ln, bias=1e-5,
                                                 scale=1.0))
                            incw(sc, sc.activation(scl[:, 3:4],
                                                    scl[:, 2:3], AF.Exp,
                                                    bias=0.0, scale=-0.5),
                                 "se")
                            W(sc, "ve", ve_b + 5 * tt + 2)
                            SS(sc, sc.activation(scl[:, 6:7], scl[:, 2:3],
                                                 AF.Ln, bias=1e-6,
                                                 scale=1.0 / D))
                            SS(sc, sc.activation(scl[:, 6:7], scl[:, 6:7],
                                                 AF.Exp, bias=0.0,
                                                 scale=-0.5))
                            SS(sc, sc.activation(scl[:, 7:8], scl[:, 5:6],
                                                 AF.Ln, bias=0.0,
                                                 scale=1.0 / D))
                            incw(sc, sc.activation(scl[:, 7:8],
                                                    scl[:, 7:8], AF.Exp,
                                                    bias=0.0, scale=0.5),
                                 "se")

            def h16(ap):
                return ap.rearrange("p (h n) -> p h n", h=16)

            def ht(ap):
                return ap.rearrange("p (h t) -> p h t", t=16)

            def colbc(ap2d, w):
                return ap2d.rearrange("p (h o) -> p h o", o=1).broadcast_to(
                    [128, 16, w])

            def segbc(seg):
                return seg.rearrange("p (o n) -> p o n", o=1).broadcast_to(
                    [128, 16, 64])

            stg_after_group = []
            for g in range(8):
                lnq_b = cur("ve")
                with nc.Block() as blk:
                    @blk.sync
                    def _(sy, g=g, lnq_b=lnq_b):
                        if g > 0:
                            W(sy, "ve", lnq_b)
                        dma(sy, "ldx",
                            xgrp[:, 0:4096].rearrange("p (t n) -> p t n",
                                                      t=4),
                            x_h.rearrange("(t p) d -> p t d", p=128)
                            [:, 4 * g:4 * (g + 1), :])

                    @blk.vector
                    def _(ve):
                        W(ve, "ldx", cur("ldx"), True)

                lnq_base = cur("ve")
                ln_quant_block(g, xgrp, xqg)

                tr_pipeline(
                    4,
                    lambda i, j: xqg[:, i * 1024 + j * 128:
                                     i * 1024 + (j + 1) * 128],
                    lambda ve, i, ps: ve.tensor_copy(
                        xqfm[:, :].rearrange("p (c n) -> p c n", c=8)
                        [:, :, i * 128:(i + 1) * 128],
                        ps.rearrange("p (c n) -> p c n", c=8)),
                    wait_fn=lambda te, i, b=lnq_base:
                        W(te, "ve", b + 5 * (i + 1)))

                pse = contextlib.ExitStack()
                with pse, nc.Block() as blk:
                    psp = [pse.enter_context(
                        nc.psum_tensor(un("psp"), [128, 512], F32))
                        for _ in range(4)]
                    te_b, ve_b = cur("te"), cur("ve")

                    @blk.tensor
                    def _(te):
                        for idx in range(24):
                            pi, rem = divmod(idx, 8)
                            tt, eh = divmod(rem, 2)
                            wT = (wqT, wkT, wvT)[pi]
                            if idx < 4:
                                W(te, "ve", ve_b)
                            else:
                                W(te, "ve", ve_b + idx - 4 + 1)
                            for dc in range(8):
                                ins = te.matmul(
                                    psp[idx % 4][:, :],
                                    xqfm[:, dc * 512 + tt * 128:
                                         dc * 512 + (tt + 1) * 128],
                                    wT[:, dc * 1024 + eh * 512:
                                       dc * 1024 + (eh + 1) * 512],
                                    start=(dc == 0), stop=(dc == 7))
                            inc(ins, "te")

                    @blk.vector
                    def _(ve):
                        for idx in range(24):
                            pi, rem = divmod(idx, 8)
                            tt, eh = divmod(rem, 2)
                            dst = (qtk, ktk, vtk)[pi]
                            W(ve, "te", te_b + idx + 1)
                            inc(ve.tensor_copy(
                                dst[:, tt * 1024 + eh * 512:
                                    tt * 1024 + (eh + 1) * 512],
                                psp[idx % 4][:, :]), "ve")

                for tt in range(4):
                    bt = g * 4 + tt
                    qti = qtk[:, tt * 1024:(tt + 1) * 1024]
                    kti = ktk[:, tt * 1024:(tt + 1) * 1024]
                    vti = vtk[:, tt * 1024:(tt + 1) * 1024]
                    pse = contextlib.ExitStack()
                    with pse, nc.Block() as blk:
                        psS = pse.enter_context(
                            nc.psum_tensor(un("psS"), [128, 8], F32))
                        psT = pse.enter_context(
                            nc.psum_tensor(un("psT"), [128, 128], BF16))
                        psR = pse.enter_context(
                            nc.psum_tensor(un("psR"), [128, 32], BF16))
                        psB = pse.enter_context(
                            nc.psum_tensor(un("psB"), [128, 16], F32))
                        psX = pse.enter_context(
                            nc.psum_tensor(un("psX"), [128, 1024], BF16))
                        ve_b, te_b, se_b = cur("ve"), cur("te"), cur("se")
                        stg_gate = (stg_after_group[g - 1]
                                    if (g > 0 and tt == 0) else None)

                        @blk.vector
                        def _(ve, tt=tt, bt=bt, qti=qti, kti=kti, vti=vti,
                              stg_gate=stg_gate):
                            for t in range(16):
                                ph = prodc[:, (t % 2) * 1024:
                                           (t % 2) * 1024 + 1024]
                                ve.tensor_tensor(
                                    h16(ph), h16(qti),
                                    segbc(kti[:, t * 64:(t + 1) * 64]),
                                    ALU.mult)
                                ins = ve.reduce_sum(
                                    ht(sqs)[:, :, t:t + 1].rearrange(
                                        "p h o -> p (h o)"),
                                    h16(ph), axis=AX)
                                if t == 15:
                                    SV(ve, ins)
                            SV(ve, ve.reduce_max(rmx[:, :], ht(sqs), axis=AX))
                            inc(ve.tensor_tensor(ht(subs), ht(sqs),
                                                 colbc(rmx, 16),
                                                 ALU.subtract), "ve")
                            W(ve, "se", se_b + 1)
                            SV(ve, ve.reduce_sum(den[:, :], ht(pexp),
                                                 axis=AX))
                            SV(ve, ve.reciprocal(rmx[:, :], den[:, :]))
                            SV(ve, ve.tensor_tensor(ht(pn), ht(pexp),
                                                    colbc(rmx, 16),
                                                    ALU.mult))
                            for t in range(16):
                                vrep = segbc(vti[:, t * 64:(t + 1) * 64])
                                pcol = ht(pn)[:, :, t:t + 1].broadcast_to(
                                    [128, 16, 64])
                                if t == 0:
                                    ve.tensor_tensor(h16(ctxu), vrep, pcol,
                                                     ALU.mult)
                                else:
                                    pc = prodc[:, (t % 2) * 1024:
                                               (t % 2) * 1024 + 1024]
                                    ve.tensor_tensor(h16(pc), vrep, pcol,
                                                     ALU.mult)
                                    ve.tensor_tensor(ctxu[:, :], ctxu[:, :],
                                                     pc, ALU.add)
                            ve.tensor_tensor(prod[:, 0:1024], ctxu[:, :],
                                             ctxu[:, :], ALU.mult)
                            SV(ve, ve.reduce_sum(ssqh[:, :],
                                                 h16(prod[:, 0:1024]),
                                                 axis=AX))
                            SV(ve, ve.reduce_max(amx_bf[:, :], h16(ctxu),
                                                 axis=AX,
                                                 apply_absolute_value=True))
                            SV(ve, ve.tensor_copy(ssq_hl[:, 0:16],
                                                  ssqh[:, :]))
                            SV(ve, ve.tensor_sub(rmx[:, :], ssqh[:, :],
                                                 ssq_hl[:, 0:16]))
                            inc(ve.tensor_copy(ssq_hl[:, 16:32], rmx[:, :]),
                                "ve")
                            W(ve, "te", te_b + 2)
                            W(ve, "se", se_b + 2)
                            SV(ve, ve.reduce_max(
                                sm16[0:16, 0:8],
                                psT[0:16, :].rearrange("p (g j) -> p g j",
                                                       j=16), axis=AX))
                            SV(ve, ve.tensor_mul(sm16[0:16, 16:24],
                                                 sm16[0:16, 0:8],
                                                 sm16[0:16, 8:16]))
                            SV(ve, ve.tensor_scalar_max(sm16[0:16, 16:24],
                                                        sm16[0:16, 16:24],
                                                        1e-5))
                            SV(ve, ve.tensor_scalar_mul(
                                sinv_all[0:16, bt * 8:bt * 8 + 8],
                                sm16[0:16, 16:24], 1.0 / 127.0))
                            SV(ve, ve.reciprocal(sm16[0:16, 24:32],
                                                 sm16[0:16, 16:24]))
                            SV(ve, ve.tensor_mul(sm16[0:16, 32:40],
                                                 sm16[0:16, 24:32],
                                                 sm16[0:16, 8:16]))
                            SV(ve, ve.tensor_scalar_mul(sm16[0:16, 40:48],
                                                        sm16[0:16, 32:40],
                                                        127.0))
                            SV(ve, ve.tensor_copy(rs_hl_bf[0:16, 0:8],
                                                  sm16[0:16, 40:48]))
                            SV(ve, ve.tensor_sub(sm16[0:16, 48:56],
                                                 sm16[0:16, 40:48],
                                                 rs_hl_bf[0:16, 0:8]))
                            inc(ve.tensor_copy(rs_hl_bf[0:16, 8:16],
                                               sm16[0:16, 48:56]), "ve")
                            W(ve, "te", te_b + 3)
                            inc(ve.tensor_copy(rsT_bf[0:8, 0:32],
                                               psR[0:8, 0:32]), "ve")
                            W(ve, "te", te_b + 4)
                            SV(ve, ve.tensor_copy(rs_tile[:, :], psB[:, :]))
                            incw(ve, ve.tensor_tensor(h16(zf), h16(ctxu),
                                                      colbc(rs_tile, 64),
                                                      ALU.mult), "ve")
                            incw(ve, ve.tensor_scalar_add(zf[:, :], zf[:, :],
                                                          RB), "ve")
                            incw(ve, ve.tensor_single_scalar(
                                zi[:, :], zi[:, :], m15[:, 0:1],
                                ALU.bitwise_and), "ve")
                            inc(ve.tensor_single_scalar(
                                xqc[:, :], zf[:, :], 384.0,
                                ALU.subtract), "ve")
                            W(ve, "te", te_b + 5)
                            if stg_gate is not None:
                                W(ve, "stg", stg_gate, True)
                            inc(ve.tensor_copy(
                                stage[:, :].rearrange("p (c n) -> p c n",
                                                      c=8)
                                [:, :, tt * 128:(tt + 1) * 128],
                                psX[:, :].rearrange("p (c n) -> p c n",
                                                    c=8)), "ve")

                        @blk.scalar
                        def _(sc):
                            W(sc, "ve", ve_b + 1)
                            inc(sc.activation(pexp[:, :], subs[:, :], AF.Exp,
                                              bias=0.0, scale=0.125), "se")
                            W(sc, "te", te_b + 1)
                            SS(sc, sc.activation(sm16[0:16, 8:16],
                                                 psS[0:16, :], AF.Ln,
                                                 bias=1e-6,
                                                 scale=1.0 / 1024.0))
                            inc(sc.activation(sm16[0:16, 8:16],
                                              sm16[0:16, 8:16], AF.Exp,
                                              bias=0.0, scale=-0.5), "se")

                        @blk.tensor
                        def _(te):
                            W(te, "ve", ve_b + 2)
                            te.matmul(psS[0:16, 0:8], ssq_hl[:, 0:16],
                                      ones16[:, :], start=True, stop=False)
                            inc(te.matmul(psS[0:16, 0:8], ssq_hl[:, 16:32],
                                          ones16[:, :], start=False,
                                          stop=True), "te")
                            inc(te.transpose(psT[0:16, :], amx_bf[:, :],
                                             ident_bf[:, :]), "te")
                            W(te, "ve", ve_b + 3)
                            te.transpose(psR[0:8, 0:16], rs_hl_bf[0:16, 0:8],
                                         ident_bf[0:16, 0:16])
                            inc(te.transpose(psR[0:8, 16:32],
                                             rs_hl_bf[0:16, 8:16],
                                             ident_bf[0:16, 0:16]), "te")
                            W(te, "ve", ve_b + 4)
                            te.matmul(psB[:, :], ones16T[0:8, :],
                                      rsT_bf[0:8, 0:16], start=True,
                                      stop=False)
                            inc(te.matmul(psB[:, :], ones16T[0:8, :],
                                          rsT_bf[0:8, 16:32], start=False,
                                          stop=True), "te")
                            W(te, "ve", ve_b + 8)
                            for dc in range(8):
                                ins = te.transpose(
                                    psX[:, dc * 128:(dc + 1) * 128],
                                    xqc[:, dc * 128:(dc + 1) * 128],
                                    ident_bf[:, :])
                            inc(ins, "te")

                with nc.Block() as blk:
                    @blk.gpsimd
                    def _(gp, g=g, ve_need=cur("ve")):
                        W(gp, "ve", ve_need)
                        dma(gp, "stg",
                            ctx_scr[:, :, g * 512:(g + 1) * 512].transpose(
                                [1, 0, 2]),
                            stage[:, :].rearrange("p (c n) -> p c n", c=8))
                stg_after_group.append(cur("stg"))

            pse = contextlib.ExitStack()
            with pse, nc.Block() as blk:
                psF = pse.enter_context(
                    nc.psum_tensor("psF", [128, 64], BF16))
                ve_b, te_b = cur("ve"), cur("te")

                @blk.vector
                def _(ve):
                    SV(ve, ve.tensor_copy(sinv_hl[0:16, 0:256],
                                          sinv_all[0:16, :]))
                    SV(ve, ve.tensor_sub(tmp16[0:16, :], sinv_all[0:16, :],
                                         sinv_hl[0:16, 0:256]))
                    inc(ve.tensor_copy(sinv_hl[0:16, 256:512],
                                       tmp16[0:16, :]), "ve")
                    W(ve, "te", te_b + 1)
                    for hc in range(2):
                        dst = sinv_rows[:, :].rearrange(
                            "p (h c) -> p h c", c=2)[:, :, hc:hc + 1]\
                            .rearrange("p h o -> p (h o)")
                        SV(ve, ve.tensor_copy(dst, psF[:, hc * 16:
                                                       (hc + 1) * 16]))
                        SV(ve, ve.tensor_tensor(dst, dst,
                                                psF[:, 32 + hc * 16:
                                                    32 + (hc + 1) * 16],
                                                ALU.add))
                    inc(ve.engine_nop(), "ve")

                @blk.tensor
                def _(te):
                    W(te, "ve", ve_b + 1)
                    for hc in range(2):
                        te.transpose(psF[:, hc * 16:(hc + 1) * 16],
                                     sinv_hl[0:16, hc * 128:(hc + 1) * 128],
                                     ident_bf[0:16, 0:16])
                    for hc in range(2):
                        ins = te.transpose(
                            psF[:, 32 + hc * 16:32 + (hc + 1) * 16],
                            sinv_hl[0:16, 256 + hc * 128:
                                    256 + (hc + 1) * 128],
                            ident_bf[0:16, 0:16])
                    inc(ins, "te")

            if dbg.get("dump_scr"):
                with nc.Block() as blk:
                    @blk.gpsimd
                    def _(gp, ve_need=cur("ve")):
                        W(gp, "ve", ve_need)
                        dma(gp, "stg", sinv_dbg[:, :], sinv_rows[:, :])

        stg_bprime = cur("stg")
        if stop == "B":
            with nc.Block() as blk:
                @blk.gpsimd
                def _(gp):
                    W(gp, "stg", cur("stg"), True)
            _chk(maxwait, cnt)
            return nc

        qst = contextlib.ExitStack()
        with qst:
            def qb(n, sh, dt):
                return qst.enter_context(nc.sbuf_tensor(n, sh, dt))

            x_qt = qb("x_qt", [128, 4096], F32)
            ctx_fm = qb("ctx_fm", [64, 4096], BF16)
            ctx_sw = qb("ctx_sw", [64, 4096], BF16)
            tok2 = qb("tok2", [128, 2048], BF16)
            xnf_fm = qb("xnf_fm", [128, 4096], BF16)
            h_qt = qb("h_qt", [128, 8192], BF16)
            hsq = qb("hsq", [128, 512], BF16)
            wup_b = qb("wup_b", [128, 4096], BF16)
            wdn_b = qb("wdn_b", [128, 4096], BF16)
            out_sb = qb("out_sb", [128, 2048], F32)
            rowt = qb("rowt", [128, 512], F32)
            tmst = qb("tmst", [128, 8], F32)

            for qt in range(NQT):
                if stop is not None and isinstance(stop, int) and qt > stop:
                    break
                with nc.Block() as blk:
                    @blk.sync
                    def _(sy, qt=qt):
                        if qt == 0:
                            W(sy, "stg", stg_bprime, True)
                        dma(sy, "ldx",
                            x_qt[:, :].rearrange("p (t n) -> p t n", t=4),
                            x_h.rearrange("(t p) d -> p t d", p=128)
                            [:, 4 * qt:4 * (qt + 1), :])
                        dma(sy, "ldk", ctx_fm[0:64, :],
                            ctx_scr[qt][0:64, :])
                        dma(sy, "ldv", ctx_sw[0:64, :],
                            ctx_scr[qt][64:128, :])
                ldx_need = cur("ldx")

                for half in range(2):
                    pse = contextlib.ExitStack()
                    with pse, nc.Block() as blk:
                        PAt = pse.enter_context(
                            nc.psum_tensor(un("PAt"), [128, 2048], BF16))
                        PAf = PAt.bitcast(F32)
                        U0 = pse.enter_context(
                            nc.psum_tensor(un("U0"), [128, 512], F32))
                        U1 = pse.enter_context(
                            nc.psum_tensor(un("U1"), [128, 512], F32))
                        H1 = pse.enter_context(
                            nc.psum_tensor(un("H1"), [128, 512], F32))
                        ve_b, te_b, se_b = cur("ve"), cur("te"), cur("se")
                        ldu_b, ldd_b = cur("ldu"), cur("ldd")
                        stg_b = cur("stg")
                        ldk_need, ldv_need = qt + 1, qt + 1

                        @blk.sync
                        def _(sy, qt=qt, half=half):
                            for gi in range(16):
                                if gi >= 2:
                                    W(sy, "te", te_b + 4 * gi - 2)
                                dma(sy, "ldu",
                                    wup_b[:, (gi % 2) * 2048:
                                          (gi % 2) * 2048 + 2048].rearrange(
                                        "p (a n) -> p a n", a=2),
                                    wup_scr[2 * gi:2 * gi + 2].transpose(
                                        [1, 0, 2]))
                            for gi in range(16):
                                if gi >= 2:
                                    W(sy, "te", te_b + 68 + 2 * gi)
                                dma(sy, "ldd",
                                    wdn_b[:, (gi % 2) * 2048:
                                          (gi % 2) * 2048 + 2048].rearrange(
                                        "p (a n) -> p a n", a=2),
                                    wdn_scr[2 * gi:2 * gi + 2].transpose(
                                        [1, 0, 2]))
                            W(sy, "stg", stg_b + 1, True)
                            dma(sy, "ldr", tmst[:, 4 + 2 * half:6 + 2 * half],
                                rows_scr[1 + half, 0:256].rearrange(
                                    "(j p) -> p j", p=128), slow=True)

                        @blk.vector
                        def _(ve, qt=qt, half=half, ldx_need=ldx_need):
                            if half == 0:
                                W(ve, "ldx", ldx_need, True)
                            for tl in range(2):
                                tt = half * 2 + tl
                                T = 4 * qt + tt
                                inc(ve.tensor_scalar(
                                    tok2[:, tl * 1024:(tl + 1) * 1024],
                                    x_qt[:, tt * 1024:(tt + 1) * 1024],
                                    sc_gxnf[:, T:T + 1], sc_bf[:, T:T + 1],
                                    ALU.mult, ALU.add), "ve")
                            for tl in range(2):
                                W(ve, "te", te_b + tl + 1)
                                hh = tl * 1024
                                inc(ve.tensor_copy(
                                    xnf_fm[:, :].rearrange(
                                        "p (c n) -> p c n", c=8)
                                    [:, :, (half * 2 + tl) * 128:
                                     (half * 2 + tl + 1) * 128],
                                    PAt[:, hh:hh + 1024].rearrange(
                                        "p (c n) -> p c n", c=8)), "ve")
                            for fc in range(32):
                                W(ve, "se", se_b + fc + 1)
                                if fc >= 2:
                                    W(ve, "te", te_b + 2 * fc)
                                hs = h_qt[:, fc * 256:(fc + 1) * 256]
                                inc(ve.tensor_mul(
                                    hsq[:, (fc % 2) * 256:
                                        (fc % 2) * 256 + 256],
                                    hs, hs), "ve")
                            W(ve, "te", te_b + 66)
                            incw(ve, ve.tensor_scalar_max(
                                rowt[0:1, 0:256],
                                H1[0:1, half * 256:half * 256 + 256], 1e-35),
                                "ve")
                            W(ve, "se", se_b + 33)
                            SV(ve, ve.tensor_scalar_add(rowt[0:1, 0:256],
                                                        rowt[0:1, 0:256],
                                                        1e-8))
                            inc(ve.reciprocal(rowt[0:1, 0:256],
                                              rowt[0:1, 0:256]), "ve")
                            W(ve, "stg", stg_b, True)
                            for tl in range(2):
                                tt = half * 2 + tl
                                Tp = 4 * qt + tt
                                for eh in range(2):
                                    W(ve, "te", te_b + 67 + 2 * tl + eh)
                                    sl = out_sb[:, tl * 1024 + eh * 512:
                                                tl * 1024 + (eh + 1) * 512]
                                    inc(ve.scalar_tensor_tensor(
                                        sl, (U0, U1)[eh][:, :],
                                        sinv_rows[:, Tp:Tp + 1],
                                        x_qt[:, tt * 1024 + eh * 512:
                                             tt * 1024 + (eh + 1) * 512],
                                        ALU.mult, ALU.add), "ve")
                            W(ve, "te", te_b + 102)
                            W(ve, "ldr", cur("ldr"), True)
                            for tl in range(2):
                                for eh in range(2):
                                    ps = (PAf[:, 0:512], PAf[:, 512:1024],
                                          U0[:, :], U1[:, :])[tl * 2 + eh]
                                    inc(ve.scalar_tensor_tensor(
                                        out_sb[:, tl * 1024 + eh * 512:
                                               tl * 1024 + (eh + 1) * 512],
                                        ps,
                                        tmst[:, 4 + 2 * half + tl:
                                              5 + 2 * half + tl],
                                        out_sb[:, tl * 1024 + eh * 512:
                                               tl * 1024 + (eh + 1) * 512],
                                        ALU.mult, ALU.add), "ve")

                        @blk.tensor
                        def _(te, qt=qt, half=half):
                            for tl in range(2):
                                W(te, "ve", ve_b + tl + 1)
                                hh = tl * 1024
                                for dc in range(8):
                                    ins = te.transpose(
                                        PAt[:, hh + dc * 128:
                                            hh + (dc + 1) * 128],
                                        tok2[:, tl * 1024 + dc * 128:
                                             tl * 1024 + (dc + 1) * 128],
                                        ident_bf[:, :])
                                inc(ins, "te")
                            W(te, "ve", ve_b + 4)
                            for fc in range(32):
                                gi = fc // 2
                                W(te, "ldu", ldu_b + gi + 1, True)
                                if fc >= 2:
                                    W(te, "se", se_b + fc - 1)
                                ub = (U0, U1)[fc % 2]
                                base = (gi % 2) * 2048 + (fc % 2) * 1024
                                for dc in range(8):
                                    ins = te.matmul(
                                        ub[:, 0:256],
                                        wup_b[:, base + dc * 128:
                                              base + (dc + 1) * 128],
                                        xnf_fm[:, dc * 512 + half * 256:
                                               dc * 512 + half * 256 + 256],
                                        start=(dc == 0), stop=(dc == 7))
                                inc(ins, "te")
                                W(te, "ve", ve_b + 4 + fc + 1)
                                inc(te.matmul(
                                    H1[0:1, half * 256:half * 256 + 256],
                                    onescol_bf[:, :],
                                    hsq[:, (fc % 2) * 256:
                                        (fc % 2) * 256 + 256],
                                    start=(fc == 0), stop=(fc == 31)), "te")
                            W(te, "se", se_b + 32)
                            W(te, "ldk", ldk_need, True)
                            W(te, "ldv", ldv_need, True)
                            W(te, "ldw", cur("ldw"), True)
                            for tl in range(2):
                                tt = half * 2 + tl
                                hh, gh = tt // 2, tt % 2
                                csrc = ctx_fm if hh == 0 else ctx_sw
                                if tl == 1:
                                    W(te, "ve", ve_b + 40)
                                for eh in range(2):
                                    ps = (U0, U1)[eh]
                                    for j in range(16):
                                        lhsT = csrc.rearrange(
                                            "p (n j) -> p n j", j=16)[
                                            0:64,
                                            gh * 128:(gh + 1) * 128,
                                            j:j + 1].rearrange(
                                            "p n o -> p (n o)")
                                        wsrc = woT if j % 2 == 0 else woT_o
                                        ins = te.matmul(
                                            ps[:, :], lhsT,
                                            wsrc[0:64,
                                                 (j // 2) * 1024 + eh * 512:
                                                 (j // 2) * 1024 +
                                                 (eh + 1) * 512],
                                            start=(j == 0),
                                            stop=(j == 15))
                                    inc(ins, "te")
                            W(te, "ve", ve_b + 42)
                            for gi in range(16):
                                W(te, "ldd", ldd_b + gi + 1, True)
                                for f2 in range(2):
                                    fc = 2 * gi + f2
                                    base = (gi % 2) * 2048 + f2 * 1024
                                    for tl in range(2):
                                        for eh in range(2):
                                            ps = (PAf[:, 0:512],
                                                  PAf[:, 512:1024],
                                                  U0[:, :], U1[:, :])[
                                                      tl * 2 + eh]
                                            ins = te.matmul(
                                                ps,
                                                h_qt[:, fc * 256 + tl * 128:
                                                     fc * 256 +
                                                     (tl + 1) * 128],
                                                wdn_b[:, base + eh * 512:
                                                      base + (eh + 1) * 512],
                                                start=(fc == 0),
                                                stop=(fc == 31))
                                    inc(ins, "te")

                        @blk.scalar
                        def _(sc, qt=qt, half=half):
                            for fc in range(32):
                                W(sc, "te", te_b + 2 + 2 * fc + 1)
                                ub = (U0, U1)[fc % 2]
                                inc(sc.activation(
                                    h_qt[:, fc * 256:(fc + 1) * 256],
                                    ub[:, 0:256], AF.Gelu, bias=0.0,
                                    scale=1.0), "se")
                            W(sc, "ve", ve_b + 37)
                            SS(sc, sc.activation(rowt[0:1, 0:256],
                                                 rowt[0:1, 0:256], AF.Ln,
                                                 bias=0.0, scale=1.0 / FF))
                            incw(sc, sc.activation(rowt[0:1, 0:256],
                                                   rowt[0:1, 0:256], AF.Exp,
                                                   bias=0.0, scale=0.5), "se")

                        @blk.gpsimd
                        def _(gp, qt=qt, half=half):
                            W(gp, "ve", ve_b + 38)
                            dma(gp, "stg", rows_scr[1 + half, 0:256],
                                rowt[0:1, 0:256])
                            for tl in range(2):
                                tt = half * 2 + tl
                                W(gp, "ve", ve_b + 44 + 2 * tl)
                                dma(gp, "stg",
                                    y_h[(4 * qt + tt) * 128:
                                        (4 * qt + tt + 1) * 128, :],
                                    out_sb[:, tl * 1024:(tl + 1) * 1024])
                            W(gp, "stg", cur("stg"), True)

            with nc.Block() as blk:
                @blk.gpsimd
                def _(gp):
                    W(gp, "stg", cur("stg"), True)

    _chk(maxwait, cnt)
    return nc


def _chk(maxwait, cnt):
    bad = {k: (v, cnt.get(k, 0)) for k, v in maxwait.items()
           if v > cnt.get(k, 0)}
    if bad:
        raise RuntimeError("wait targets exceed final counts: %r" % bad)


_NC_CACHE = []


def kernel(L0_attn_x, L0_attn_wq, L0_attn_wk, L0_attn_wv, L0_attn_wo,
           L0_ffn_w_up, L0_ffn_w_down):
    if not _NC_CACHE:
        _NC_CACHE.append(build_nc())
    nc = _NC_CACHE[0]
    x = np.ascontiguousarray(L0_attn_x)
    in_maps = []
    for c in range(8):
        in_maps.append({
            "x": np.ascontiguousarray(x[c]),
            "wq": np.ascontiguousarray(L0_attn_wq),
            "wk": np.ascontiguousarray(L0_attn_wk),
            "wv": np.ascontiguousarray(L0_attn_wv),
            "wo": np.ascontiguousarray(L0_attn_wo),
            "wup": np.ascontiguousarray(L0_ffn_w_up),
            "wdn": np.ascontiguousarray(L0_ffn_w_down),
        })
    res = run_bass_kernel_spmd(nc, in_maps, core_ids=list(range(8)))
    return np.stack([r["y"] for r in res.results], axis=0)
